# version 02
# baseline (speedup 1.0000x reference)
import numpy as np
import concourse.bass as bass
import concourse.mybir as mybir
from concourse.bass_utils import run_bass_kernel_spmd

F32 = mybir.dt.float32
BF16 = mybir.dt.bfloat16
AF = mybir.ActivationFunctionType
ALU = mybir.AluOpType
ts = bass.ts
ds = bass.ds
AXX = mybir.AxisListType.X

S, D, N, R = 8192, 1024, 16, 64
TC = 64
NCHUNK = S // TC
NPAIR = NCHUNK // 2

V_INCS = 9
S_INCS = 9
PE_INCS = 2
OFF_SV = 3
PREP_INCS = 11
DMAS_PER_CHUNK = 8

DC2 = 1.0 + 1e-8
DC1 = -2.0 + 2e-8
DC0 = 1.0 + 1e-8
DC1_OVER_DC2 = DC1 / DC2
INV_D = 1.0 / D



def host_constants(h):
    phases = np.linspace(0.0, 2.0 * np.pi * (1.0 - 1.0 / N), N)
    ph = phases[8 * h:8 * h + 8]
    c1 = np.zeros((128, 672), np.float64)
    for k in range(128):
        for m in range(128):
            if k // 16 == m // 16:
                c1[k, m] = INV_D
    for p in range(128):
        c1[p, 128 + (p % 16)] = 1.0
    c1[:, 144] = 1e-10
    c1[:, 146] = 1e-8
    c1[:, 147] = 1e-5
    r_of_p = np.arange(128) // 16
    h0r = 0.01 * np.cos(ph)[r_of_p]
    h0i = 0.01 * np.sin(ph)[r_of_p]
    c1[:, 148:212] = (h0r + 1e-10)[:, None]
    c1[:, 212:276] = h0i[:, None]
    tsq0 = h0r * h0r + h0i * h0i
    c1[:, 276:340] = tsq0[:, None]
    c1[:, 340] = 64.0 * h0r
    c1[:, 341] = 64.0 * h0i
    c1[:, 342:406] = np.sqrt(tsq0 + 1e-8)[:, None]
    c1[:, 406:534] = np.eye(128)
    c1[0, 534:662] = 1.0
    c1[0, 662] = 1e-10
    c1[:, 664] = -0.25
    c1[:, 665] = 1.25
    return np.ascontiguousarray(c1.astype(np.float32))


def host_trig():
    idx = np.arange(D, dtype=np.int64)
    angm = 2.0 * np.pi * ((np.outer(idx, idx) % D).astype(np.float64)) / D
    return (np.ascontiguousarray(np.cos(angm).astype(np.float32)),
            np.ascontiguousarray(np.sin(angm).astype(np.float32)))


_TRIG = None


def make_inputs(core, full):
    global _TRIG
    if _TRIG is None:
        _TRIG = host_trig()
    trigC, trigS = _TRIG
    b, h = core // 2, core % 2
    c1 = host_constants(h)
    Wx = np.asarray(full["L0_attn_W_xproj"])
    bcols = [64 + (8 * h + nl) * 2 + cc for nl in range(8) for cc in range(2)]
    ccols = [96 + (8 * h + nl) * 2 + cc for nl in range(8) for cc in range(2)]
    rcols = [(8 * h + nl) * 2 + cc for nl in range(8) for cc in range(2)]
    w_xe = np.concatenate([Wx[:, 0:64], Wx[:, bcols], Wx[:, ccols],
                           np.asarray(full["L0_attn_W_res"])[:, rcols]], axis=1)
    f32 = lambda a: np.ascontiguousarray(np.asarray(a, np.float32))
    return {
        "x": f32(np.asarray(full["L0_attn_x"])[b]),
        "w_in": f32(full["L0_attn_W_in"]),
        "w_xe": f32(w_xe),
        "w_dt": f32(full["L0_attn_W_dt"]),
        "b_dt_t": f32(np.asarray(full["L0_attn_b_dt"]).reshape(8, 128).T),
        "w_out": f32(full["L0_attn_W_out"]),
        "b_out_t": f32(np.asarray(full["L0_attn_b_out"]).reshape(8, 128).T),
        "wrq": f32(full["L0_ffn_w_freq_real"]),
        "wiq": f32(full["L0_ffn_w_freq_imag"]),
        "fb_t": f32(np.asarray(full["L0_ffn_bias"]).reshape(8, 128).T),
        "a_log": f32(np.repeat(np.asarray(full["L0_attn_A_log"],
                                          np.float32)[8 * h:8 * h + 8, None], 2, 1)),
        "trc": trigC, "trs": trigS, "cons1": c1,
    }



class Seq:
    def __init__(self):
        self.ops = []
        self.pos = 0

    def on(self, eng, fn, dma=False):
        inc = 16 if dma else 1
        self.ops.append((eng, fn, self.pos, inc))
        self.pos += inc

    def emit(self, engname, eng, gsem):
        for e, fn, pos, inc in self.ops:
            if e != engname:
                continue
            eng.wait_ge(gsem, pos)
            fn(eng).then_inc(gsem, inc)


class Mono:

    def __init__(self, nc, eng, sem, init, period):
        self.nc, self.sem = nc, sem
        self.regs = nc.alloc_registers(None, engines=[eng.engine])
        nc.regs_mov(self.regs, init)
        self.init = init
        self.period = period
        self.cur = init

    def wait(self, eng, target):
        d = target - self.cur
        assert d >= 0, (self.cur, target)
        if d > 0:
            self.nc.regs_alu(self.regs, self.regs, d, op=ALU.add)
            self.cur = target
        eng.wait_ge(self.sem, self.regs)

    def check(self):
        assert self.cur == self.init + self.period, (self.cur, self.init,
                                                     self.period)


def step_bulk(cb, k):
    a3 = cb["alph"].rearrange("p (dl tt) -> p dl tt", dl=64)
    u3 = cb["uRv"].rearrange("p (dl tt) -> p dl tt", dl=64)
    return {
        "xr2": cb["XR2"][:, 2 * k:2 * k + 2],
        "mrx": cb["mrx"][:, k:k + 1],
        "mrxp": cb["mrxp"][:, k:k + 1],
        "alph": a3[:, :, k:k + 1],
        "u": u3[:, :, k:k + 1],
        "Br": cb["Brv"][:, k:k + 1], "Bi": cb["Biv"][:, k:k + 1],
        "Cr": cb["Crv"][:, k:k + 1], "Ci": cb["Civ"][:, k:k + 1],
    }

def emit_step_v(v, t, cons, spe, bulk, sv, k, pyt, ch, m_ss, m_spe, m_pg):
    STT = v.scalar_tensor_tensor
    TT = v.tensor_tensor
    TS = v.tensor_scalar
    JUNK = lambda: TS(cons["JD"], cons["JS"], 1.0, None, ALU.mult)
    if k == 0:
        m_pg.wait(v, (ch + 1) * PREP_INCS + 4)
    m_spe.wait(v, ch * TC * PE_INCS + k * PE_INCS + 2)
    v.tensor_copy(t["hcs"][:], t["ps_hc"][:, 0:2])
    STT(t["LNA"][:, 0:64], t["H"][:, 0:64], 2e-10, t["tsq"][:], ALU.mult,
        ALU.add)
    STT(t["scr2"][:], bulk["xr2"], 1.0, t["ps_hc"][:, 0:2], ALU.mult,
        ALU.mult, accum_out=t["dotv"][:])
    STT(t["scr2b"][:], t["hcs"][:], 1.0, t["hcs"][:], ALU.mult,
        ALU.mult, accum_out=t["LNA"][:, 64:65]).then_inc(sv, 1)
    m_ss.wait(v, k * S_INCS + 3)
    TT(t["cg1"][:], t["dotv"][:], t["rn"][:], ALU.mult)
    JUNK()
    JUNK()
    STT(t["gn"][:], t["cg1"][:], bulk["mrxp"], cons["CM025"], ALU.mult,
        ALU.add)
    STT(t["gp1"][:], t["cg1"][:], bulk["mrx"], cons["CP125"], ALU.mult,
        ALU.add)
    JUNK()
    TS(t["ag1"][:], bulk["alph"], t["gn"][:, 0:1], None, ALU.mult)
    JUNK()
    TS(t["ag1"][:], t["ag1"][:], t["gp1"][:, 0:1], None, ALU.add)
    TT(t["Xt"][:], t["ag1"][:], t["lg"][:], ALU.mult)
    STT(t["pw"][:], t["LNAo"][:, 0:64], -0.5, t["Xt"][:], ALU.mult,
        ALU.add).then_inc(sv, 1)
    m_ss.wait(v, k * S_INCS + 4)
    TT(t["H2"][:, 0:64], t["H"][:, 0:64], t["s2"][:], ALU.mult)
    TT(t["H2"][:, 64:128], t["H"][:, 64:128], t["s2"][:], ALU.mult)
    STT(t["H3"][:, 0:64], bulk["u"], bulk["Br"], t["H2"][:, 0:64],
        ALU.mult, ALU.add)
    STT(t["H3"][:, 64:128], bulk["u"], bulk["Bi"], t["H2"][:, 64:128],
        ALU.mult, ALU.add)
    TS(t["H3"][:, 0:64], t["H3"][:, 0:64], 1e-10, None,
       ALU.add).then_inc(sv, 1)
    TT(t["SQ"][:, 0:64], t["H3"][:, 0:64], t["H3"][:, 0:64], ALU.mult)
    TT(t["SQ"][:, 64:128], t["H3"][:, 64:128], t["H3"][:, 64:128], ALU.mult)
    STT(t["LNB"][:, 128:192], t["SQ"][:, 0:64], 1.0, t["SQ"][:, 64:128],
        ALU.mult, ALU.add)
    JUNK()
    STT(t["qcmt"][:], t["H3"][:, 0:64], -2e-10, t["LNB"][:, 128:192],
        ALU.mult, ALU.add).then_inc(sv, 1)
    m_ss.wait(v, k * S_INCS + 7)
    TS(t["sD"][:], t["e2"][:], DC1_OVER_DC2, None, ALU.add)
    TS(t["LNB"][:, 0:64], t["e2"][:], 1.0, None, ALU.add)
    TT(t["qD"][:], t["e2"][:], t["sD"][:], ALU.mult)
    JUNK()
    TS(t["LNB"][:, 64:128], t["qD"][:], DC2, DC0, ALU.mult,
       ALU.add).then_inc(sv, 1)
    m_ss.wait(v, k * S_INCS + 8)
    STT(t["EXC"][:, 0:64], t["LNBo"][:, 128:192], -0.5, t["LNBo"][:, 0:64],
        ALU.mult, ALU.subtract)
    STT(t["EXC"][:, 64:128], t["LNBo"][:, 64:128], 0.5, t["LNBo"][:, 0:64],
        ALU.mult, ALU.subtract)
    TS(t["EXC"][:, 128:192], t["LNBo"][:, 0:64], -1.0, None,
       ALU.mult).then_inc(sv, 1)
    m_ss.wait(v, k * S_INCS + 9)
    STT(t["s3"][:], t["e2"][:], -1.0, t["EXPC"][:, 0:64], ALU.add, ALU.mult)
    STT(t["tvt"][:], t["e2"][:], -1.0, t["EXPC"][:, 128:192], ALU.add,
        ALU.mult)
    STT(t["H"][:, 0:64], t["H3"][:, 0:64], 1.0, t["s3"][:], ALU.mult,
        ALU.mult, accum_out=t["crci"][:, 0:1])
    STT(t["H"][:, 64:128], t["H3"][:, 64:128], 1.0, t["s3"][:], ALU.mult,
        ALU.mult, accum_out=t["crci"][:, 1:2]).then_inc(sv, 1)
    TS(t["py0"][:], t["H"][:, 0:64], bulk["Cr"], None, ALU.mult)
    TS(t["H"][:, 0:64], t["H"][:, 0:64], 1e-10, None, ALU.add)
    STT(pyt[:], t["H"][:, 64:128], bulk["Ci"], t["py0"][:], ALU.mult,
        ALU.add).then_inc(sv, 1)
    TT(t["tsq"][:], t["tvt"][:], t["tvt"][:], ALU.mult).then_inc(sv, 1)

def emit_step_s(s, t, ss, k, m_sv):
    s.activation(out=t["lg"][:], in_=t["EXPC"][:, 64:128], func=AF.Ln,
                 bias=1e-8).then_inc(ss, 1)
    m_sv.wait(s, k * V_INCS + 1 + OFF_SV)
    s.activation(out=t["LNAo"][:, 0:65], in_=t["LNA"][:, 0:65],
                 func=AF.Ln).then_inc(ss, 1)
    s.activation(out=t["rn"][:], in_=t["LNAo"][:, 64:65], func=AF.Exp,
                 scale=-0.5).then_inc(ss, 1)
    m_sv.wait(s, k * V_INCS + 2 + OFF_SV)
    s.activation(out=t["s2"][:], in_=t["pw"][:], func=AF.Exp).then_inc(ss, 1)
    m_sv.wait(s, k * V_INCS + 4 + OFF_SV)
    s.activation(out=t["lA"][:], in_=t["qcmt"][:], func=AF.Ln,
                 bias=1e-8).then_inc(ss, 1)
    s.activation(out=t["cmA"][:], in_=t["lA"][:], func=AF.Exp,
                 scale=0.5).then_inc(ss, 1)
    s.activation(out=t["e2"][:], in_=t["cmA"][:], func=AF.Exp,
                 scale=2.0).then_inc(ss, 1)
    m_sv.wait(s, k * V_INCS + 5 + OFF_SV)
    s.activation(out=t["LNBo"][:], in_=t["LNB"][:],
                 func=AF.Ln).then_inc(ss, 1)
    m_sv.wait(s, k * V_INCS + 6 + OFF_SV)
    s.activation(out=t["EXPC"][:], in_=t["EXC"][:],
                 func=AF.Exp).then_inc(ss, 1)




def build_nc(dbg=False, npair=NPAIR):
    nc = bass.Bass("TRN2", target_bir_lowering=False, debug=False)

    x_in = nc.dram_tensor("x", [S, D], F32, kind="ExternalInput").ap()
    w_in = nc.dram_tensor("w_in", [D, 2 * D], F32, kind="ExternalInput").ap()
    w_xe = nc.dram_tensor("w_xe", [D, 112], F32, kind="ExternalInput").ap()
    w_dt = nc.dram_tensor("w_dt", [R, D], F32, kind="ExternalInput").ap()
    b_dt_t = nc.dram_tensor("b_dt_t", [128, 8], F32, kind="ExternalInput").ap()
    w_out = nc.dram_tensor("w_out", [D, D], F32, kind="ExternalInput").ap()
    b_out_t = nc.dram_tensor("b_out_t", [128, 8], F32, kind="ExternalInput").ap()
    wr_in = nc.dram_tensor("wrq", [D, D], F32, kind="ExternalInput").ap()
    wi_in = nc.dram_tensor("wiq", [D, D], F32, kind="ExternalInput").ap()
    fb_t = nc.dram_tensor("fb_t", [128, 8], F32, kind="ExternalInput").ap()
    a_log = nc.dram_tensor("a_log", [8, 2], F32, kind="ExternalInput").ap()
    trc = nc.dram_tensor("trc", [D, D], F32, kind="ExternalInput").ap()
    trs = nc.dram_tensor("trs", [D, D], F32, kind="ExternalInput").ap()
    cons1 = nc.dram_tensor("cons1", [128, 672], F32, kind="ExternalInput").ap()
    out = nc.dram_tensor("out", [S, D], F32, kind="ExternalOutput").ap()

    st_xlnT = nc.dram_tensor("st_xlnT", [D, S], F32, kind="Internal").ap()
    st_xz = nc.dram_tensor("st_xz", [2 * D, S], F32, kind="Internal").ap()
    st_xin = nc.dram_tensor("st_xin", [D, S], F32, kind="Internal").ap()
    st_xp = nc.dram_tensor("st_xp", [112, S + 2 * TC], F32, kind="Internal").ap()
    st_sz = nc.dram_tensor("st_sz", [D, S], F32, kind="Internal").ap()
    st_dt = nc.dram_tensor("st_dt", [D, S + 2 * TC], F32, kind="Internal").ap()
    st_u = nc.dram_tensor("st_u", [D, S + 2 * TC], F32, kind="Internal").ap()
    y_part = nc.dram_tensor("y_part", [D, S], F32, kind="Internal").ap()
    y_cc = nc.dram_tensor("y_cc", [D, S], F32, kind="Internal").ap()
    st_mt = nc.dram_tensor("st_mt", [D, D], F32, kind="Internal").ap()
    dbg_outs = {}
    if dbg:
        for nm, shape in (("d_xz", [2048, 512]), ("d_xp", [112, 512]),
                          ("d_dt", [1024, 512]), ("d_u", [1024, 512]),
                          ("d_sz", [1024, 512]), ("d_y", [1024, 512]),
                          ("d_ycc", [1024, 512]), ("d_mt", [1024, 512]),
                          ("d_xln", [1024, 512])):
            dbg_outs[nm] = nc.dram_tensor(nm, shape, F32, kind="ExternalOutput").ap()
        dbg_outs["d_Htr"] = nc.dram_tensor("d_Htr", [128, 128 * 130], F32,
                                           kind="ExternalOutput").ap()
        for nm2, w in (("H", 128), ("tsq", 64), ("crci", 2), ("EXPC", 192),
                       ("s3", 64), ("e2", 64), ("LNB", 192), ("LNBo", 192),
                       ("qcmt", 64), ("s2", 64), ("pw", 64), ("lg", 64),
                       ("LNA", 66), ("LNAo", 66), ("dotv", 1), ("rn", 1),
                       ("pyt0", 64), ("pyt1", 64)):
            dbg_outs["ds_" + nm2] = nc.dram_tensor(
                "ds_" + nm2, [128, w], F32, kind="ExternalOutput").ap()

    t = {}
    ctxs = []

    def sb(name, shape, dtype=F32):
        hx = nc.sbuf_tensor(name, shape, dtype)
        t[name] = hx.__enter__()
        ctxs.append(hx)

    def psal(name, shape):
        hx = nc.psum_tensor(name, shape, F32)
        t[name] = hx.__enter__()
        ctxs.append(hx)

    sb("aF", [128, 16384])
    sb("a2F", [128, 4096])
    sb("aB", [128, 16384])
    sb("xlT_bf", [128, 4096])
        
    sb("cons1t", [128, 672])
    sb("anegv", [128, 2])
    sb("alog", [8, 2])
    sb("wxe_bf", [128, 8 * 112])
    sb("wdt_bf", [64, 1024])
    sb("bdt_sb", [128, 8])
    sb("bout_sb", [128, 8])
    sb("fb_sb", [128, 8])
    sb("mvt", [128, 4])
    sb("rstd", [128, 1])
    sb("trw0", [128, 128])
    sb("trw1", [128, 128])
    
    sb("H", [128, 128]); sb("tsq", [128, 64]); sb("crci", [128, 2])
    sb("hcs", [128, 2]); sb("scr2", [128, 2]); sb("scr2b", [128, 2])
    sb("dotv", [128, 1]); sb("rn", [128, 1]); sb("cg1", [128, 1])
    sb("gnn", [128, 1]); sb("gn", [128, 1]); sb("gp1", [128, 1])
    sb("LNA", [128, 66]); sb("LNAo", [128, 66]); sb("lg", [128, 64])
    sb("ag1", [128, 64]); sb("Xt", [128, 64]); sb("pw", [128, 64])
    sb("s2", [128, 64]); sb("H2", [128, 128]); sb("H3", [128, 128])
    sb("SQ", [128, 128]); sb("qcmt", [128, 64]); sb("lA", [128, 64])
    sb("cmA", [128, 64]); sb("e2", [128, 64]); sb("sD", [128, 64])
    sb("qD", [128, 64]); sb("LNB", [128, 192]); sb("LNBo", [128, 192])
    sb("EXC", [128, 192]); sb("EXPC", [128, 192]); sb("s3", [128, 64])
    sb("tvt", [128, 64]); sb("py0", [128, 64]); sb("jnkp", [128, 64])
    sb("pyt0", [128, 64]); sb("pyt1", [128, 64])
    sb("pxa", [128, 64]); sb("pxb", [128, 64]); sb("pxc", [128, 64])
    for pr in ("A", "B"):
        sb(f"Brv{pr}", [128, TC]); sb(f"Biv{pr}", [128, TC])
        sb(f"Crv{pr}", [128, TC]); sb(f"Civ{pr}", [128, TC])
        sb(f"XRr{pr}", [128, TC]); sb(f"XRi{pr}", [128, TC])
        sb(f"XR2{pr}", [128, 2 * TC])
        sb(f"mrx{pr}", [128, TC])
        sb(f"mrxp{pr}", [128, TC])
    sb("ySB0", [16, 4096])

    psal("ps_hc", [128, 2]); psal("ps_y0", [16, 512]); psal("ps_y1", [16, 512])
    psal("ps_mm0", [128, 512]); psal("ps_mm1", [128, 512]); psal("ps_xp", [112, 512])

    aF = t["aF"]
    a2F = t["a2F"]
    aB = t["aB"]

    big0 = aF[:, 0:4096]
    big1 = aF[:, 4096:8192]
    xt0 = aF[:, 8192:9216]
    xc0 = aF[:, 9216:10240]
    xlnf = aF[:, 10240:11264]
    sqt = aF[:, 11264:12288]
    evac = aF[:, 12288:12800]
    evb = aF[:, 12288:13312]
    silp3 = aF[:, 8192:12288]
    silf = aF[:, 12800:13312]
    xp_ev = aF[0:112, 13312:13824]
    xpblk = aF[0:64, 13824:14336]
    dtt = aF[:, 14336:14848]
    xint = aF[:, 14848:15360]
    ut = aF[:, 15360:15872]

    uRv = {"A": aF[:, 0:4096], "B": aF[:, 4096:8192]}
    alph = {"A": aF[:, 8192:12288], "B": aF[:, 12288:16384]}

    EtS = lambda kc: aF[:, kc * 1024:(kc + 1) * 1024]
    FtS = lambda kc: aF[:, 8192 + kc * 1024:8192 + (kc + 1) * 1024]
    wrk = a2F[:, 0:1024]; wik = a2F[:, 1024:2048]
    ck = a2F[:, 2048:3072]; sk = a2F[:, 3072:4096]

    att8 = aF[:, 0:4096]
    xmid = aF[:, 4096:8192]
    ff8 = aF[:, 8192:12288]
    ygk = aF[:, 12288:12800]
    szk = aF[:, 12800:13312]
    xt3 = aF[:, 13312:14336]
    xc3 = aF[:, 14336:15360]
    xl3 = aF[:, 15360:16384]

    win_bf = aB
    wout_bf = aB[:, 0:8192]
    mt_bf = aB[:, 8192:16384]
    xl2b = t["xlT_bf"]

    ps_tr = t["ps_mm1"][:, 0:128]

    with (
        nc.semaphore("g") as g,
        nc.semaphore("svA") as svA, nc.semaphore("svB") as svB,
        nc.semaphore("ssA") as ssA, nc.semaphore("ssB") as ssB,
        nc.semaphore("spe") as spe,
        nc.semaphore("pg") as pg,
        nc.semaphore("sye") as sye,
        nc.semaphore("sdi") as sdi,
        nc.semaphore("sdy") as sdy,
        nc.semaphore("g2") as g2,
        nc.Block() as block,
    ):
        c1t = t["cons1t"]
        Wmean = c1t[:, 0:128]
        W4 = c1t[:, 128:144]
        epsv01 = c1t[:, 144:146]
        ident = c1t[:, 406:534]
        cons_v = {"CM025": c1t[:, 664:665], "CP125": c1t[:, 665:666],
                  "JS": c1t[:, 148:212], "JD": t["jnkp"][:]}
        nc.const_aps.register(F32, 1e-8, c1t[:, 146:147])
        nc.const_aps.register(F32, 1e-5, c1t[:, 147:148])

        sq = Seq()
        sq.on("sync", lambda e: e.dma_start(out=c1t[:], in_=cons1[:]), dma=True)
        sq.on("sync", lambda e: e.dma_start(out=t["bdt_sb"][:], in_=b_dt_t[:]), dma=True)
        sq.on("sync", lambda e: e.dma_start(out=t["bout_sb"][:], in_=b_out_t[:]), dma=True)
        sq.on("sync", lambda e: e.dma_start(out=t["fb_sb"][:], in_=fb_t[:]), dma=True)
        sq.on("sync", lambda e: e.dma_start(out=t["alog"][:], in_=a_log[:]), dma=True)
        for kc in range(8):
            sq.on("gpsimd", lambda e, kc=kc: e.dma_start(
                out=win_bf[:, ts(kc, 2048)], in_=w_in[ts(kc, 128), :]), dma=True)
            sq.on("gpsimd", lambda e, kc=kc: e.dma_start(
                out=t["wxe_bf"][:, ts(kc, 112)], in_=w_xe[ts(kc, 128), :]), dma=True)
        sq.on("gpsimd", lambda e: e.dma_start(out=t["wdt_bf"][:], in_=w_dt[:]), dma=True)
        sq.on("gpsimd", lambda e: e.dma_start(
            out=t["anegv"][:],
            in_=a_log.rearrange("r (o c) -> r o c", o=1).broadcast_to([8, 16, 2])),
            dma=True)
        sq.on("scalar", lambda e: e.activation(out=t["anegv"][:], in_=t["anegv"][:],
                                               func=AF.Exp))
        sq.on("vector", lambda e: e.tensor_scalar(t["anegv"][:], t["anegv"][:], -1.0,
                                                  None, ALU.mult))
        sq.on("vector", lambda e: e.tensor_copy(t["H"][:, 0:64], c1t[:, 148:212]))
        sq.on("vector", lambda e: e.tensor_copy(t["H"][:, 64:128], c1t[:, 212:276]))
        sq.on("vector", lambda e: e.tensor_copy(t["tsq"][:], c1t[:, 276:340]))
        sq.on("vector", lambda e: e.tensor_copy(t["crci"][:], c1t[:, 340:342]))
        sq.on("vector", lambda e: e.tensor_copy(t["EXPC"][:, 64:128], c1t[:, 342:406]))

        for i in range(64):
            sq.on("sync", lambda e, i=i: e.dma_start(
                out=xt0, in_=x_in[ts(i, 128), :]), dma=True)
            sq.on("vector", lambda e: e.reduce_sum(t["mvt"][:, 0:1], xt0, axis=AXX))
            sq.on("vector", lambda e: e.tensor_scalar(t["mvt"][:, 1:2], t["mvt"][:, 0:1],
                                                      INV_D, None, ALU.mult))
            sq.on("vector", lambda e: e.tensor_scalar(xc0, xt0, t["mvt"][:, 1:2],
                                                      None, ALU.subtract))
            sq.on("scalar", lambda e: e.activation(out=sqt, in_=xc0, func=AF.Square))
            sq.on("vector", lambda e: e.reduce_sum(t["mvt"][:, 2:3], sqt, axis=AXX))
            sq.on("vector", lambda e: e.tensor_scalar(t["mvt"][:, 3:4], t["mvt"][:, 2:3],
                                                      INV_D, None, ALU.mult))
            sq.on("scalar", lambda e: e.activation(out=t["rstd"][:], in_=t["mvt"][:, 3:4],
                                                   func=AF.Ln, bias=1e-5))
            sq.on("scalar", lambda e: e.activation(out=t["rstd"][:], in_=t["rstd"][:],
                                                   func=AF.Exp, scale=-0.5))
            sq.on("vector", lambda e: e.tensor_scalar(xlnf, xc0, t["rstd"][:, 0:1],
                                                      None, ALU.mult))
            for kc in range(8):
                sq.on("tensor", lambda e, kc=kc: e.transpose(
                    ps_tr, xlnf[:, ts(kc, 128)], ident))
                sq.on("vector", lambda e, kc=kc: e.tensor_copy(
                    evb[:, ts(kc, 128)], ps_tr))
            sq.on("sync", lambda e, i=i: e.dma_start(
                out=st_xlnT.rearrange("(kc p) s -> p kc s", p=128)[:, :, ts(i, 128)],
                in_=evb.rearrange("p (kc c) -> p kc c", kc=8)), dma=True)

        for blk in range(16):
            sq.on("sync", lambda e, blk=blk: e.dma_start(
                out=t["xlT_bf"][:].rearrange("p (kc c) -> p kc c", kc=8),
                in_=st_xlnT.rearrange("(kc p) s -> p kc s", p=128)[:, :, ts(blk, 512)]),
                dma=True)
            for mc in range(16):
                def mmfn(e, mc=mc):
                    for kc in range(8):
                        inst = e.matmul(
                            t["ps_mm0"][:],
                            win_bf[:, kc * 2048 + mc * 128:kc * 2048 + (mc + 1) * 128],
                            t["xlT_bf"][:, ts(kc, 512)],
                            start=(kc == 0), stop=(kc == 7))
                    return inst
                sq.on("tensor", mmfn)
                sq.on("vector", lambda e: e.tensor_copy(evac, t["ps_mm0"][:]))
                sq.on("sync", lambda e, blk=blk, mc=mc: e.dma_start(
                    out=st_xz[ts(mc, 128), ts(blk, 512)], in_=evac), dma=True)

        for blk in range(16):
            sq.on("sync", lambda e, blk=blk: e.dma_start(
                out=big0.rearrange("p (kc c) -> p kc c", kc=8),
                in_=st_xz.rearrange("(kc p) s -> p kc s", p=128)[:, 0:8, ts(blk, 512)]),
                dma=True)
            for kc in range(8):
                sq.on("scalar", lambda e, kc=kc: e.activation(
                    out=big0[:, ts(kc, 512)], in_=big0[:, ts(kc, 512)],
                    func=AF.Silu))
                sq.on("sync", lambda e, blk=blk, kc=kc: e.dma_start(
                    out=st_xin[ts(kc, 128), ts(blk, 512)],
                    in_=big0[:, ts(kc, 512)]), dma=True)
            def xpfn(e):
                for kc in range(8):
                    inst = e.matmul(t["ps_xp"][:], t["wxe_bf"][:, ts(kc, 112)],
                                    big0[:, ts(kc, 512)],
                                    start=(kc == 0), stop=(kc == 7))
                return inst
            sq.on("tensor", xpfn)
            sq.on("vector", lambda e: e.tensor_copy(xp_ev, t["ps_xp"][:]))
            sq.on("sync", lambda e, blk=blk: e.dma_start(
                out=st_xp[:, ts(blk, 512)], in_=xp_ev), dma=True)
            sq.on("sync", lambda e, blk=blk: e.dma_start(
                out=big1.rearrange("p (kc c) -> p kc c", kc=8),
                in_=st_xz.rearrange("(kc p) s -> p kc s", p=128)[:, 8:16, ts(blk, 512)]),
                dma=True)
            for kc in range(8):
                sq.on("scalar", lambda e, kc=kc: e.activation(
                    out=silf, in_=big1[:, ts(kc, 512)], func=AF.Silu))
                sq.on("sync", lambda e, blk=blk, kc=kc: e.dma_start(
                    out=st_sz[ts(kc, 128), ts(blk, 512)], in_=silf), dma=True)

        for blk in range(16):
            sq.on("gpsimd", lambda e, blk=blk: e.dma_start(
                out=xpblk, in_=st_xp[0:64, ts(blk, 512)]), dma=True)
            for mc in range(8):
                sq.on("tensor", lambda e, mc=mc: e.matmul(
                    t["ps_mm0"][:], t["wdt_bf"][:, ts(mc, 128)], xpblk,
                    start=True, stop=True))
                sq.on("scalar", lambda e, mc=mc: e.activation(
                    out=dtt, in_=t["ps_mm0"][:], func=AF.Exp,
                    bias=t["bdt_sb"][:, mc:mc + 1]))
                sq.on("vector", lambda e: e.tensor_scalar(
                    dtt, dtt, 1.0, None, ALU.add))
                sq.on("scalar", lambda e: e.activation(
                    out=dtt, in_=dtt, func=AF.Ln))
                sq.on("sync", lambda e, blk=blk, mc=mc: e.dma_start(
                    out=xint, in_=st_xin[ts(mc, 128), ts(blk, 512)]), dma=True)
                sq.on("vector", lambda e: e.tensor_tensor(ut, dtt, xint, ALU.mult))
                sq.on("sync", lambda e, blk=blk, mc=mc: e.dma_start(
                    out=st_dt[ts(mc, 128), ts(blk, 512)], in_=dtt), dma=True)
                sq.on("sync", lambda e, blk=blk, mc=mc: e.dma_start(
                    out=st_u[ts(mc, 128), ts(blk, 512)], in_=ut), dma=True)
        sq.on("vector", lambda e: e.memset(ut, 0.0))
        for mc in range(8):
            sq.on("sync", lambda e, mc=mc: e.dma_start(
                out=st_u[ts(mc, 128), S:S + 2 * TC], in_=ut[:, 0:128]), dma=True)
            sq.on("sync", lambda e, mc=mc: e.dma_start(
                out=st_dt[ts(mc, 128), S:S + 2 * TC], in_=ut[:, 0:128]), dma=True)
        sq.on("sync", lambda e: e.dma_start(
            out=st_xp[:, S:S + 2 * TC], in_=ut[0:112, 0:128]), dma=True)
        G_P1 = sq.pos

        def cbufs(pr):
            d = {k: t[f"{k}{pr}"] for k in
                 ("Brv", "Biv", "Crv", "Civ", "XRr", "XRi", "XR2", "mrx",
                  "mrxp")}
            d["uRv"] = uRv[pr]
            d["alph"] = alph[pr]
            return d

        def chunk_dmas(e, cb, che):
            e.dma_start(
                out=cb["uRv"],
                in_=st_u.rearrange("(dh dl) s -> dh dl s", dh=16)[
                    :, :, ds(che * TC, TC)].partition_broadcast(8)).then_inc(sdi, 16)
            e.dma_start(
                out=cb["alph"],
                in_=st_dt.rearrange("(dh dl) s -> dh dl s", dh=16)[
                    :, :, ds(che * TC, TC)].partition_broadcast(8)).then_inc(sdi, 16)
            for i, (nm, row) in enumerate((("Brv", 64), ("Biv", 65), ("Crv", 80),
                                           ("Civ", 81), ("XRr", 96), ("XRi", 97))):
                rend = (row // 16) * 16 + 16
                e.dma_start(
                    out=cb[nm][:],
                    in_=st_xp[row:rend:2, ds(che * TC, TC)].rearrange(
                        "r (o c) -> r o c", o=1).broadcast_to([8, 16, TC])
                ).then_inc(sdi, 16)

        def prep_ops(cb):
            return [
                ("scalar", lambda e: e.activation(out=cb["alph"], in_=cb["alph"],
                                                  func=AF.Exp,
                                                  scale=t["anegv"][:, 0:1])),
                ("vector", lambda e: e.tensor_scalar(cb["XRr"][:], cb["XRr"][:],
                                                     1e-10, None, ALU.add)),
                ("vector", lambda e: e.tensor_tensor(t["pxa"][:], cb["XRr"][:],
                                                     cb["XRr"][:], ALU.mult)),
                ("vector", lambda e: e.tensor_tensor(t["pxb"][:], cb["XRi"][:],
                                                     cb["XRi"][:], ALU.mult)),
                ("vector", lambda e: e.tensor_tensor(t["pxa"][:], t["pxa"][:],
                                                     t["pxb"][:], ALU.add)),
                ("scalar", lambda e: e.activation(out=t["pxc"][:], in_=t["pxa"][:],
                                                  func=AF.Ln)),
                ("scalar", lambda e: e.activation(out=t["pxc"][:], in_=t["pxc"][:],
                                                  func=AF.Exp, scale=-0.5)),
                ("vector", lambda e: e.tensor_scalar(cb["mrx"][:], t["pxc"][:],
                                                     -0.25, None, ALU.mult)),
                ("vector", lambda e: e.tensor_scalar(cb["mrxp"][:], t["pxc"][:],
                                                     0.25, None, ALU.mult)),
                ("vector", lambda e: e.tensor_copy(cb["XR2"][:, 0:2 * TC:2],
                                                   cb["XRr"][:])),
                ("vector", lambda e: e.tensor_copy(cb["XR2"][:, 1:2 * TC:2],
                                                   cb["XRi"][:])),
            ]

        def emit_prep(engname, eng, cb, base, dma_target, m_pg, m_sdi):
            first = True
            for i, (e_, fn) in enumerate(prep_ops(cb)):
                if e_ != engname:
                    continue
                if first:
                    m_sdi.wait(eng, dma_target)
                    first = False
                m_pg.wait(eng, base + i + 4)
                fn(eng).then_inc(pg, 1)

        sq2 = Seq()
        for kc in range(8):
            sq2.on("sync", lambda e, kc=kc: e.dma_start(
                out=wrk, in_=wr_in[ts(kc, 128), :]), dma=True)
            sq2.on("sync", lambda e, kc=kc: e.dma_start(
                out=wik, in_=wi_in[ts(kc, 128), :]), dma=True)
            sq2.on("sync", lambda e, kc=kc: e.dma_start(
                out=ck, in_=trc[ts(kc, 128), :]), dma=True)
            sq2.on("sync", lambda e, kc=kc: e.dma_start(
                out=sk, in_=trs[ts(kc, 128), :]), dma=True)
            for mc in range(8):
                for fc in range(2):
                    def etfn(e, mc=mc, fc=fc):
                        e.matmul(t["ps_mm0"][:], wrk[:, ts(mc, 128)],
                                 ck[:, ts(fc, 512)], start=True, stop=False)
                        return e.matmul(t["ps_mm0"][:], wik[:, ts(mc, 128)],
                                        sk[:, ts(fc, 512)], start=False, stop=True)
                    sq2.on("tensor", etfn)
                    if kc == 0:
                        sq2.on("vector", lambda e, mc=mc, fc=fc: e.tensor_copy(
                            EtS(mc)[:, ts(fc, 512)], t["ps_mm0"][:]))
                    else:
                        sq2.on("vector", lambda e, mc=mc, fc=fc: e.tensor_tensor(
                            EtS(mc)[:, ts(fc, 512)], EtS(mc)[:, ts(fc, 512)],
                            t["ps_mm0"][:], ALU.add))
            sq2.on("vector", lambda e: e.tensor_scalar(wik, wik, -1.0, None, ALU.mult))
            for mc in range(8):
                for fc in range(2):
                    def ftfn(e, mc=mc, fc=fc):
                        e.matmul(t["ps_mm0"][:], wrk[:, ts(mc, 128)],
                                 sk[:, ts(fc, 512)], start=True, stop=False)
                        return e.matmul(t["ps_mm0"][:], wik[:, ts(mc, 128)],
                                        ck[:, ts(fc, 512)], start=False, stop=True)
                    sq2.on("tensor", ftfn)
                    if kc == 0:
                        sq2.on("vector", lambda e, mc=mc, fc=fc: e.tensor_copy(
                            FtS(mc)[:, ts(fc, 512)], t["ps_mm0"][:]))
                    else:
                        sq2.on("vector", lambda e, mc=mc, fc=fc: e.tensor_tensor(
                            FtS(mc)[:, ts(fc, 512)], FtS(mc)[:, ts(fc, 512)],
                            t["ps_mm0"][:], ALU.add))
        for mc in range(8):
            for fc in range(2):
                for kc in range(8):
                    sq2.on("sync", lambda e, mc=mc, kc=kc: e.dma_start(
                        out=t["trw0"][:], in_=trc[ts(kc, 128), ts(mc, 128)]), dma=True)
                    sq2.on("sync", lambda e, mc=mc, kc=kc: e.dma_start(
                        out=t["trw1"][:], in_=trs[ts(kc, 128), ts(mc, 128)]), dma=True)
                    def mtfn(e, fc=fc, kc=kc):
                        e.matmul(t["ps_mm0"][:], t["trw0"][:], EtS(kc)[:, ts(fc, 512)],
                                 start=(kc == 0), stop=False)
                        return e.matmul(t["ps_mm0"][:], t["trw1"][:],
                                        FtS(kc)[:, ts(fc, 512)], start=False,
                                        stop=(kc == 7))
                    sq2.on("tensor", mtfn)
                sq2.on("scalar", lambda e: e.activation(
                    out=a2F[:, 0:512], in_=t["ps_mm0"][:], func=AF.Identity,
                    scale=INV_D))
                sq2.on("sync", lambda e, mc=mc, fc=fc: e.dma_start(
                    out=st_mt[ts(mc, 128), ts(fc, 512)], in_=a2F[:, 0:512]),
                    dma=True)
        for bi in range(8):
            for bj in range(8):
                sq2.on("sync", lambda e, bi=bi, bj=bj: e.dma_start(
                    out=t["trw0"][:], in_=st_mt[ts(bi, 128), ts(bj, 128)]), dma=True)
                sq2.on("tensor", lambda e: e.transpose(ps_tr, t["trw0"][:], ident))
                sq2.on("vector", lambda e, bi=bi, bj=bj: e.tensor_copy(
                    mt_bf[:, bj * 1024 + bi * 128:bj * 1024 + (bi + 1) * 128], ps_tr))
        sq2.on("gpsimd", lambda e: e.collective_compute(
            "AllReduce", ALU.add, ins=[y_part[:]], outs=[y_cc[:]],
            replica_groups=[[0, 1], [2, 3], [4, 5], [6, 7]]))
        for kc in range(8):
            sq2.on("gpsimd", lambda e, kc=kc: e.dma_start(
                out=wout_bf[:, ts(kc, 1024)], in_=w_out[ts(kc, 128), :]), dma=True)
        for blk in range(16):
            for kc in range(8):
                sq2.on("sync", lambda e, blk=blk, kc=kc: e.dma_start(
                    out=ygk, in_=y_cc[ts(kc, 128), ts(blk, 512)]), dma=True)
                sq2.on("sync", lambda e, blk=blk, kc=kc: e.dma_start(
                    out=szk, in_=st_sz[ts(kc, 128), ts(blk, 512)]), dma=True)
                sq2.on("vector", lambda e, kc=kc: e.tensor_tensor(
                    silp3[:, ts(kc, 512)], ygk, szk, ALU.mult))
            for mc in range(8):
                def wofn(e, mc=mc):
                    for kc in range(8):
                        inst = e.matmul(
                            t["ps_mm0"][:],
                            wout_bf[:, kc * 1024 + mc * 128:kc * 1024 + (mc + 1) * 128],
                            silp3[:, ts(kc, 512)], start=(kc == 0), stop=(kc == 7))
                    return inst
                sq2.on("tensor", wofn)
                sq2.on("scalar", lambda e, mc=mc: e.activation(
                    out=att8[:, ts(mc, 512)], in_=t["ps_mm0"][:], func=AF.Identity,
                    bias=t["bout_sb"][:, mc:mc + 1]))
            for tk in range(4):
                sq2.on("sync", lambda e, blk=blk, tk=tk: e.dma_start(
                    out=xt3, in_=x_in[blk * 512 + tk * 128:blk * 512 + (tk + 1) * 128,
                                      :]), dma=True)
                for kc in range(8):
                    sq2.on("tensor", lambda e, tk=tk, kc=kc: e.transpose(
                        ps_tr, att8[:, kc * 512 + tk * 128:kc * 512 + (tk + 1) * 128], ident))
                    sq2.on("vector", lambda e, tk=tk, kc=kc: e.tensor_tensor(
                        xmid[:, tk * 1024 + kc * 128:tk * 1024 + (kc + 1) * 128], ps_tr,
                        xt3[:, ts(kc, 128)], ALU.add))
                xmv = xmid[:, ts(tk, 1024)]
                sq2.on("vector", lambda e, xmv=xmv: e.reduce_sum(
                    t["mvt"][:, 0:1], xmv, axis=AXX))
                sq2.on("vector", lambda e: e.tensor_scalar(
                    t["mvt"][:, 1:2], t["mvt"][:, 0:1], INV_D, None, ALU.mult))
                sq2.on("vector", lambda e, xmv=xmv: e.tensor_scalar(
                    xc3, xmv, t["mvt"][:, 1:2], None, ALU.subtract))
                sq2.on("scalar", lambda e: e.activation(out=xl3, in_=xc3,
                                                        func=AF.Square))
                sq2.on("vector", lambda e: e.reduce_sum(t["mvt"][:, 2:3], xl3,
                                                        axis=AXX))
                sq2.on("vector", lambda e: e.tensor_scalar(
                    t["mvt"][:, 3:4], t["mvt"][:, 2:3], INV_D, None, ALU.mult))
                sq2.on("scalar", lambda e: e.activation(
                    out=t["rstd"][:], in_=t["mvt"][:, 3:4], func=AF.Ln, bias=1e-5))
                sq2.on("scalar", lambda e: e.activation(
                    out=t["rstd"][:], in_=t["rstd"][:], func=AF.Exp, scale=-0.5))
                sq2.on("vector", lambda e: e.tensor_scalar(
                    xl3, xc3, t["rstd"][:, 0:1], None, ALU.mult))
                for kc in range(8):
                    sq2.on("tensor", lambda e, kc=kc: e.transpose(
                        ps_tr, xl3[:, kc * 128:(kc + 1) * 128], ident))
                    sq2.on("vector", lambda e, tk=tk, kc=kc: e.tensor_copy(
                        xl2b[:, kc * 512 + tk * 128:kc * 512 + (tk + 1) * 128], ps_tr))
            for mc in range(8):
                def fffn(e, mc=mc):
                    for kc in range(8):
                        inst = e.matmul(
                            t["ps_mm0"][:],
                            mt_bf[:, kc * 1024 + mc * 128:kc * 1024 + (mc + 1) * 128],
                            xl2b[:, ts(kc, 512)], start=(kc == 0), stop=(kc == 7))
                    return inst
                sq2.on("tensor", fffn)
                sq2.on("scalar", lambda e, mc=mc: e.activation(
                    out=ff8[:, ts(mc, 512)], in_=t["ps_mm0"][:], func=AF.Identity,
                    bias=t["fb_sb"][:, mc:mc + 1]))
            for tk in range(4):
                for kc in range(8):
                    sq2.on("tensor", lambda e, tk=tk, kc=kc: e.transpose(
                        ps_tr, ff8[:, kc * 512 + tk * 128:kc * 512 + (tk + 1) * 128], ident))
                    sq2.on("vector", lambda e, tk=tk, kc=kc: e.tensor_tensor(
                        xl3[:, kc * 128:(kc + 1) * 128], ps_tr,
                        xmid[:, tk * 1024 + kc * 128:tk * 1024 + (kc + 1) * 128], ALU.add))
                sq2.on("gpsimd", lambda e, blk=blk, tk=tk: e.dma_start(
                    out=out[blk * 512 + tk * 128:blk * 512 + (tk + 1) * 128, :],
                    in_=xl3), dma=True)
        G_POST = sq2.pos

        @block.sync
        def _(sync):
            sq.emit("sync", sync, g)
            sync.wait_ge(g, G_P1)
            chunk_dmas(sync, cbufs("A"), 0)
            chunk_dmas(sync, cbufs("B"), 1)
            mA = Mono(nc, sync, svA, 3, TC * V_INCS)
            mB = Mono(nc, sync, svB, 3, TC * V_INCS)
            with sync.Fori(0, npair) as cp:
                mA.wait(sync, TC * V_INCS + OFF_SV)
                chunk_dmas(sync, cbufs("A"), cp * 2 + 2)
                mB.wait(sync, TC * V_INCS + OFF_SV)
                chunk_dmas(sync, cbufs("B"), cp * 2 + 3)
            mA.check(); mB.check()
            sq2.emit("sync", sync, g2)
            sync.wait_ge(g2, G_POST)

        @block.vector
        def _(v):
            sq.emit("vector", v, g)
            v.wait_ge(g, G_P1)
            for sem, n in ((svA, OFF_SV), (svB, OFF_SV), (spe, 1), (pg, 4),
                           (sye, 1)):
                for _ in range(n):
                    v.memset(t["scr2"][:, 0:1], 0.0).then_inc(sem, 1)
            m_ssA = Mono(nc, v, ssA, 0, TC * S_INCS)
            m_ssB = Mono(nc, v, ssB, 0, TC * S_INCS)
            m_spe = Mono(nc, v, spe, 0, 2 * TC * PE_INCS)
            m_pg = Mono(nc, v, pg, 4, 2 * PREP_INCS)
            m_sdi = Mono(nc, v, sdi, 0, 2 * DMAS_PER_CHUNK * 16)
            with v.Fori(0, npair) as cp:
                for par in (0, 1):
                    sv = svA if par == 0 else svB
                    cb = cbufs("A" if par == 0 else "B")
                    ch = par
                    emit_prep("vector", v, cb, ch * PREP_INCS,
                              (ch + 1) * DMAS_PER_CHUNK * 16, m_pg, m_sdi)
                    m_ss = m_ssA if par == 0 else m_ssB
                    for k in range(TC):
                        emit_step_v(v, t, cons_v, spe, step_bulk(cb, k), sv, k,
                                    t["pyt0"] if k % 2 == 0 else t["pyt1"],
                                    ch, m_ss, m_spe, m_pg)
            for m in (m_ssA, m_ssB, m_spe, m_pg, m_sdi):
                m.check()
            v.wait_ge(sdi, (2 * npair + 2) * DMAS_PER_CHUNK * 16)
            v.wait_ge(pg, 2 * npair * PREP_INCS + 4)
            sq2.emit("vector", v, g2)
            v.wait_ge(g2, G_POST)

        @block.scalar
        def _(s):
            sq.emit("scalar", s, g)
            s.wait_ge(g, G_P1)
            m_svA = Mono(nc, s, svA, 0, TC * V_INCS)
            m_svB = Mono(nc, s, svB, 0, TC * V_INCS)
            m_speS = Mono(nc, s, spe, 2, 2 * TC * PE_INCS)
            m_sdyS = Mono(nc, s, sdy, 16, 32)
            m_pgS = Mono(nc, s, pg, 0, 2 * PREP_INCS)
            m_sdiS = Mono(nc, s, sdi, 0, 2 * DMAS_PER_CHUNK * 16)
            with s.Fori(0, npair) as cp:
                for par in (0, 1):
                    ss = ssA if par == 0 else ssB
                    ch = par
                    emit_prep("scalar", s, cbufs("A" if par == 0 else "B"),
                              ch * PREP_INCS, (ch + 1) * DMAS_PER_CHUNK * 16,
                              m_pgS, m_sdiS)
                    m_sv = m_svA if par == 0 else m_svB
                    ysb = t["ySB0"]
                    for k in range(TC):
                        emit_step_s(s, t, ss, k, m_sv)
                        if k % 8 == 7:
                            s8 = k // 8
                            m_speS.wait(s, ch * TC * PE_INCS
                                        + (k + 1) * PE_INCS + 2)
                            if s8 == 0:
                                m_sdyS.wait(s, 16 * ch + 32)
                            s.activation(
                                out=ysb[:].rearrange("dh (dl t) -> dh dl t",
                                                     dl=64)[:, :,
                                                            s8 * 8:(s8 + 1) * 8],
                                in_=t["ps_y" + str(s8 % 2)][:].rearrange(
                                    "dh (t dl) -> dh dl t", t=8),
                                func=AF.Identity).then_inc(sye, 1)
                m_pgS.wait(s, 2 * PREP_INCS)
            for m in (m_svA, m_svB, m_speS, m_sdyS, m_pgS, m_sdiS):
                m.check()
            sq2.emit("scalar", s, g2)
            s.wait_ge(g2, G_POST)

        @block.tensor
        def _(pe):
            sq.emit("tensor", pe, g)
            pe.wait_ge(g, G_P1)
            m_svAp = Mono(nc, pe, svA, 2, TC * V_INCS)
            m_svBp = Mono(nc, pe, svB, 2, TC * V_INCS)
            m_syeP = Mono(nc, pe, sye, 0, 2 * 8)
            pe.matmul(t["ps_hc"][:, 0:2], Wmean, t["crci"][:],
                      start=True, stop=False)
            pe.matmul(t["ps_hc"][:, 0:2], c1t[0:1, 534:662],
                      c1t[0:1, 662:664], start=False,
                      stop=True).then_inc(spe, 1)
            with pe.Fori(0, npair) as cp:
                for par in (0, 1):
                    ch = par
                    m_sv = m_svAp if par == 0 else m_svBp
                    for k in range(TC):
                        gslot = ch * 8 + k // 8
                        pyt = t["pyt0"] if k % 2 == 0 else t["pyt1"]
                        psy = t["ps_y0"] if (k // 8) % 2 == 0 else t["ps_y1"]
                        m_sv.wait(pe, k * V_INCS + 7 + OFF_SV)
                        pe.matmul(t["ps_hc"][:, 0:2], Wmean, t["crci"][:],
                                  start=True, stop=False)
                        pe.matmul(t["ps_hc"][:, 0:2], c1t[0:1, 534:662],
                                  c1t[0:1, 662:664], start=False,
                                  stop=True).then_inc(spe, 1)
                        m_sv.wait(pe, k * V_INCS + 8 + OFF_SV)
                        m_syeP.wait(pe, gslot + 1)
                        pe.matmul(psy[:, ts(k % 8, 64)], W4, pyt[:],
                                  start=(k % 8 == 0), stop=(k % 8 == 7)
                                  ).then_inc(spe, 1)
            for m in (m_svAp, m_svBp, m_syeP):
                m.check()
            sq2.emit("tensor", pe, g2)
            pe.wait_ge(g2, G_POST)

        @block.gpsimd
        def _(gp):
            sq.emit("gpsimd", gp, g)
            gp.wait_ge(g, G_P1)
            gp.dma_start(out=t["mvt"][:, 0:2], in_=cons1[:, 144:146]).then_inc(sdy, 16)
            gp.dma_start(out=t["mvt"][:, 2:4], in_=cons1[:, 144:146]).then_inc(sdy, 16)
            m_syeG = Mono(nc, gp, sye, 1, 2 * 8)
            with gp.Fori(0, npair) as cp:
                for par in (0, 1):
                    ch = par
                    ysb = t["ySB0"]
                    m_syeG.wait(gp, (ch + 1) * 8 + 1)
                    gp.dma_start(
                        out=y_part.rearrange("(dh dl) s -> dh dl s", dh=16)[
                            :, :, ds((cp * 2 + ch) * TC, TC)],
                        in_=ysb[:]
                    ).then_inc(sdy, 16)
                    if dbg:
                        gp.dma_start(
                            out=dbg_outs["d_Htr"][:, ds((cp * 2 + ch) * 128, 128)],
                            in_=t["H"][:]).then_inc(sdi, 16)
            m_syeG.check()
            gp.wait_ge(sdy, 32 + 2 * npair * 16)
            sq2.emit("gpsimd", gp, g2)
            gp.wait_ge(g2, G_POST)
            if dbg:
                dsem = sdi
                nd = 0
                for nm, src_ap in (("d_xz", st_xz[:, 0:512]),
                                   ("d_xp", st_xp[:, 0:512]),
                                   ("d_dt", st_dt[:, 0:512]),
                                   ("d_u", st_u[:, 0:512]),
                                   ("d_sz", st_sz[:, 0:512]),
                                   ("d_y", y_part[:, 0:512]),
                                   ("d_ycc", y_cc[:, 0:512]),
                                   ("d_mt", st_mt[:, 0:512])):
                    gp.dma_start(out=dbg_outs[nm][:], in_=src_ap).then_inc(dsem, 16)
                    nd += 1
                gp.dma_start(out=dbg_outs["d_xln"][:],
                             in_=st_xlnT[:, 0:512]).then_inc(dsem, 16)
                nd += 1
                for nm2 in ("H", "tsq", "crci", "EXPC", "s3",
                            "e2", "LNB", "LNBo", "qcmt", "s2",
                            "pw", "lg", "LNA", "LNAo", "dotv",
                            "rn", "pyt0", "pyt1"):
                    gp.dma_start(out=dbg_outs["ds_" + nm2][:],
                                 in_=t[nm2][:]).then_inc(dsem, 16)
                    nd += 1
                gp.wait_ge(dsem, (NCHUNK + 2) * DMAS_PER_CHUNK * 16 + nd * 16)

    for hx in reversed(ctxs):
        hx.__exit__(None, None, None)
    return nc


_NC_CACHE = None


def kernel(**inputs):
    global _NC_CACHE
    if _NC_CACHE is None:
        _NC_CACHE = build_nc()
    in_maps = [make_inputs(c, inputs) for c in range(8)]
    res = run_bass_kernel_spmd(_NC_CACHE, in_maps, core_ids=list(range(8)))
    outp = np.zeros((4, S, D), np.float32)
    for b in range(4):
        outp[b] = res.results[2 * b]["out"]
    return outp


# version 03
# speedup vs baseline: 1.0248x; 1.0248x over previous
import numpy as np
import concourse.bass as bass
import concourse.mybir as mybir
from concourse.bass_utils import run_bass_kernel_spmd

F32 = mybir.dt.float32
BF16 = mybir.dt.bfloat16
AF = mybir.ActivationFunctionType
ALU = mybir.AluOpType
ts = bass.ts
ds = bass.ds
AXX = mybir.AxisListType.X

S, D, N, R = 8192, 1024, 16, 64
TC = 64
NCHUNK = S // TC
NPAIR = NCHUNK // 2

V_INCS = 9
S_INCS = 9
PE_INCS = 2
OFF_SV = 3
PREP_INCS = 11
DMAS_PER_CHUNK = 8

DC2 = 1.0 + 1e-8
DC1 = -2.0 + 2e-8
DC0 = 1.0 + 1e-8
DC1_OVER_DC2 = DC1 / DC2
INV_D = 1.0 / D



def host_constants(h):
    phases = np.linspace(0.0, 2.0 * np.pi * (1.0 - 1.0 / N), N)
    ph = phases[8 * h:8 * h + 8]
    c1 = np.zeros((128, 672), np.float64)
    for k in range(128):
        for m in range(128):
            if k // 16 == m // 16:
                c1[k, m] = INV_D
    for p in range(128):
        c1[p, 128 + (p % 16)] = 1.0
    c1[:, 144] = 1e-10
    c1[:, 146] = 1e-8
    c1[:, 147] = 1e-5
    r_of_p = np.arange(128) // 16
    h0r = 0.01 * np.cos(ph)[r_of_p]
    h0i = 0.01 * np.sin(ph)[r_of_p]
    c1[:, 148:212] = (h0r + 1e-10)[:, None]
    c1[:, 212:276] = h0i[:, None]
    tsq0 = h0r * h0r + h0i * h0i
    c1[:, 276:340] = tsq0[:, None]
    c1[:, 340] = 64.0 * h0r
    c1[:, 341] = 64.0 * h0i
    c1[:, 342:406] = np.sqrt(tsq0 + 1e-8)[:, None]
    c1[:, 406:534] = np.eye(128)
    c1[0, 534:662] = 1.0
    c1[0, 662] = 1e-10
    c1[:, 664] = -0.25
    c1[:, 665] = 1.25
    return np.ascontiguousarray(c1.astype(np.float32))


def host_trig():
    idx = np.arange(D, dtype=np.int64)
    angm = 2.0 * np.pi * ((np.outer(idx, idx) % D).astype(np.float64)) / D
    return (np.ascontiguousarray(np.cos(angm).astype(np.float32)),
            np.ascontiguousarray(np.sin(angm).astype(np.float32)))


_TRIG = None


def make_inputs(core, full):
    global _TRIG
    if _TRIG is None:
        _TRIG = host_trig()
    trigC, trigS = _TRIG
    b, h = core // 2, core % 2
    c1 = host_constants(h)
    Wx = np.asarray(full["L0_attn_W_xproj"])
    bcols = [64 + (8 * h + nl) * 2 + cc for nl in range(8) for cc in range(2)]
    ccols = [96 + (8 * h + nl) * 2 + cc for nl in range(8) for cc in range(2)]
    rcols = [(8 * h + nl) * 2 + cc for nl in range(8) for cc in range(2)]
    w_xe = np.concatenate([Wx[:, 0:64], Wx[:, bcols], Wx[:, ccols],
                           np.asarray(full["L0_attn_W_res"])[:, rcols]], axis=1)
    f32 = lambda a: np.ascontiguousarray(np.asarray(a, np.float32))
    return {
        "x": f32(np.asarray(full["L0_attn_x"])[b]),
        "w_in": f32(full["L0_attn_W_in"]),
        "w_xe": f32(w_xe),
        "w_dt": f32(full["L0_attn_W_dt"]),
        "b_dt_t": f32(np.asarray(full["L0_attn_b_dt"]).reshape(8, 128).T),
        "w_out": f32(full["L0_attn_W_out"]),
        "b_out_t": f32(np.asarray(full["L0_attn_b_out"]).reshape(8, 128).T),
        "wrq": f32(full["L0_ffn_w_freq_real"]),
        "wiq": f32(full["L0_ffn_w_freq_imag"]),
        "fb_t": f32(np.asarray(full["L0_ffn_bias"]).reshape(8, 128).T),
        "a_log": f32(np.repeat(np.asarray(full["L0_attn_A_log"],
                                          np.float32)[8 * h:8 * h + 8, None], 2, 1)),
        "trc": trigC, "trs": trigS, "cons1": c1,
    }



class Seq:
    def __init__(self):
        self.ops = []
        self.pos = 0

    def on(self, eng, fn, dma=False):
        inc = 16 if dma else 1
        self.ops.append((eng, fn, self.pos, inc))
        self.pos += inc

    def emit(self, engname, eng, gsem):
        for e, fn, pos, inc in self.ops:
            if e != engname:
                continue
            eng.wait_ge(gsem, pos)
            fn(eng).then_inc(gsem, inc)


class Mono:

    def __init__(self, nc, eng, sem, init, period):
        self.nc, self.sem = nc, sem
        self.regs = nc.alloc_registers(None, engines=[eng.engine])
        nc.regs_mov(self.regs, init)
        self.init = init
        self.period = period
        self.cur = init

    def wait(self, eng, target):
        d = target - self.cur
        assert d >= 0, (self.cur, target)
        if d > 0:
            self.nc.regs_alu(self.regs, self.regs, d, op=ALU.add)
            self.cur = target
        eng.wait_ge(self.sem, self.regs)

    def check(self):
        assert self.cur == self.init + self.period, (self.cur, self.init,
                                                     self.period)


def step_bulk(cb, k):
    a3 = cb["alph"].rearrange("p (dl tt) -> p dl tt", dl=64)
    u3 = cb["uRv"].rearrange("p (dl tt) -> p dl tt", dl=64)
    return {
        "xr2": cb["XR2"][:, 2 * k:2 * k + 2],
        "mrx": cb["mrx"][:, k:k + 1],
        "mrxp": cb["mrxp"][:, k:k + 1],
        "alph": a3[:, :, k:k + 1],
        "u": u3[:, :, k:k + 1],
        "Br": cb["Brv"][:, k:k + 1], "Bi": cb["Biv"][:, k:k + 1],
        "Cr": cb["Crv"][:, k:k + 1], "Ci": cb["Civ"][:, k:k + 1],
    }

def emit_step_v(v, t, cons, spe, bulk, sv, k, pyt, ch, m_ss, m_spe, m_pg):
    STT = v.scalar_tensor_tensor
    TT = v.tensor_tensor
    TS = v.tensor_scalar
    JUNK = lambda: TS(cons["JD"], cons["JS"], 1.0, None, ALU.mult)
    if k == 0:
        m_pg.wait(v, (ch + 1) * PREP_INCS + 4)
    m_spe.wait(v, ch * TC * PE_INCS + k * PE_INCS + 2)
    v.tensor_copy(t["hcs"][:], t["ps_hc"][:, 0:2])
    STT(t["LNA"][:, 0:64], t["H"][:, 0:64], 2e-10, t["tsq"][:], ALU.mult,
        ALU.add)
    STT(t["scr2"][:], bulk["xr2"], 1.0, t["ps_hc"][:, 0:2], ALU.mult,
        ALU.mult, accum_out=t["dotv"][:])
    STT(t["scr2b"][:], t["hcs"][:], 1.0, t["hcs"][:], ALU.mult,
        ALU.mult, accum_out=t["LNA"][:, 64:65]).then_inc(sv, 1)
    m_ss.wait(v, k * S_INCS + 3)
    TT(t["cg1"][:], t["dotv"][:], t["rn"][:], ALU.mult)
    JUNK()
    JUNK()
    STT(t["gn"][:], t["cg1"][:], bulk["mrxp"], cons["CM025"], ALU.mult,
        ALU.add)
    STT(t["gp1"][:], t["cg1"][:], bulk["mrx"], cons["CP125"], ALU.mult,
        ALU.add)
    JUNK()
    STT(t["ag1"][:], bulk["alph"], t["gn"][:, 0:1], t["lg"][:], ALU.mult,
        ALU.mult)
    STT(t["Xt"][:], t["lg"][:], t["gp1"][:, 0:1], t["ag1"][:], ALU.mult,
        ALU.add)
    STT(t["pw"][:], t["LNAo"][:, 0:64], -0.5, t["Xt"][:], ALU.mult,
        ALU.add).then_inc(sv, 1)
    m_ss.wait(v, k * S_INCS + 4)
    TT(t["H2"][:, 0:64], t["H"][:, 0:64], t["s2"][:], ALU.mult)
    TT(t["H2"][:, 64:128], t["H"][:, 64:128], t["s2"][:], ALU.mult)
    STT(t["H3"][:, 0:64], bulk["u"], bulk["Br"], t["H2"][:, 0:64],
        ALU.mult, ALU.add)
    STT(t["H3"][:, 64:128], bulk["u"], bulk["Bi"], t["H2"][:, 64:128],
        ALU.mult, ALU.add)
    TS(t["H3"][:, 0:64], t["H3"][:, 0:64], 1e-10, None,
       ALU.add).then_inc(sv, 1)
    TT(t["SQ"][:, 0:64], t["H3"][:, 0:64], t["H3"][:, 0:64], ALU.mult)
    TT(t["SQ"][:, 64:128], t["H3"][:, 64:128], t["H3"][:, 64:128], ALU.mult)
    STT(t["LNB"][:, 128:192], t["SQ"][:, 0:64], 1.0, t["SQ"][:, 64:128],
        ALU.mult, ALU.add)
    JUNK()
    STT(t["qcmt"][:], t["H3"][:, 0:64], -2e-10, t["LNB"][:, 128:192],
        ALU.mult, ALU.add).then_inc(sv, 1)
    m_ss.wait(v, k * S_INCS + 7)
    TS(t["sD"][:], t["e2"][:], DC1_OVER_DC2, None, ALU.add)
    TS(t["LNB"][:, 0:64], t["e2"][:], 1.0, None, ALU.add)
    TT(t["qD"][:], t["e2"][:], t["sD"][:], ALU.mult)
    JUNK()
    TS(t["LNB"][:, 64:128], t["qD"][:], DC2, DC0, ALU.mult,
       ALU.add).then_inc(sv, 1)
    m_ss.wait(v, k * S_INCS + 8)
    STT(t["EXC"][:, 0:64], t["LNBo"][:, 128:192], -0.5, t["LNBo"][:, 0:64],
        ALU.mult, ALU.subtract)
    STT(t["EXC"][:, 64:128], t["LNBo"][:, 64:128], 0.5, t["LNBo"][:, 0:64],
        ALU.mult, ALU.subtract)
    TS(t["EXC"][:, 128:192], t["LNBo"][:, 0:64], -1.0, None,
       ALU.mult).then_inc(sv, 1)
    m_ss.wait(v, k * S_INCS + 9)
    STT(t["s3"][:], t["e2"][:], -1.0, t["EXPC"][:, 0:64], ALU.add, ALU.mult)
    STT(t["H"][:, 0:64], t["H3"][:, 0:64], 1.0, t["s3"][:], ALU.mult,
        ALU.mult, accum_out=t["crci"][:, 0:1])
    STT(t["H"][:, 64:128], t["H3"][:, 64:128], 1.0, t["s3"][:], ALU.mult,
        ALU.mult, accum_out=t["crci"][:, 1:2]).then_inc(sv, 1)
    STT(t["tvt"][:], t["e2"][:], -1.0, t["EXPC"][:, 128:192], ALU.add,
        ALU.mult)
    TS(t["py0"][:], t["H"][:, 0:64], bulk["Cr"], None, ALU.mult)
    TS(t["H"][:, 0:64], t["H"][:, 0:64], 1e-10, None, ALU.add)
    STT(pyt[:], t["H"][:, 64:128], bulk["Ci"], t["py0"][:], ALU.mult,
        ALU.add).then_inc(sv, 1)
    TT(t["tsq"][:], t["tvt"][:], t["tvt"][:], ALU.mult).then_inc(sv, 1)

def emit_step_s(s, t, ss, k, m_sv):
    s.activation(out=t["lg"][:], in_=t["EXPC"][:, 64:128], func=AF.Ln,
                 bias=1e-8).then_inc(ss, 1)
    m_sv.wait(s, k * V_INCS + 1 + OFF_SV)
    s.activation(out=t["LNAo"][:, 0:65], in_=t["LNA"][:, 0:65],
                 func=AF.Ln).then_inc(ss, 1)
    s.activation(out=t["rn"][:], in_=t["LNAo"][:, 64:65], func=AF.Exp,
                 scale=-0.5).then_inc(ss, 1)
    m_sv.wait(s, k * V_INCS + 2 + OFF_SV)
    s.activation(out=t["s2"][:], in_=t["pw"][:], func=AF.Exp).then_inc(ss, 1)
    m_sv.wait(s, k * V_INCS + 4 + OFF_SV)
    s.activation(out=t["lA"][:], in_=t["qcmt"][:], func=AF.Ln,
                 bias=1e-8).then_inc(ss, 1)
    s.activation(out=t["cmA"][:], in_=t["lA"][:], func=AF.Exp,
                 scale=0.5).then_inc(ss, 1)
    s.activation(out=t["e2"][:], in_=t["cmA"][:], func=AF.Exp,
                 scale=2.0).then_inc(ss, 1)
    m_sv.wait(s, k * V_INCS + 5 + OFF_SV)
    s.activation(out=t["LNBo"][:], in_=t["LNB"][:],
                 func=AF.Ln).then_inc(ss, 1)
    m_sv.wait(s, k * V_INCS + 6 + OFF_SV)
    s.activation(out=t["EXPC"][:], in_=t["EXC"][:],
                 func=AF.Exp).then_inc(ss, 1)




def build_nc(dbg=False, npair=NPAIR):
    nc = bass.Bass("TRN2", target_bir_lowering=False, debug=False)

    x_in = nc.dram_tensor("x", [S, D], F32, kind="ExternalInput").ap()
    w_in = nc.dram_tensor("w_in", [D, 2 * D], F32, kind="ExternalInput").ap()
    w_xe = nc.dram_tensor("w_xe", [D, 112], F32, kind="ExternalInput").ap()
    w_dt = nc.dram_tensor("w_dt", [R, D], F32, kind="ExternalInput").ap()
    b_dt_t = nc.dram_tensor("b_dt_t", [128, 8], F32, kind="ExternalInput").ap()
    w_out = nc.dram_tensor("w_out", [D, D], F32, kind="ExternalInput").ap()
    b_out_t = nc.dram_tensor("b_out_t", [128, 8], F32, kind="ExternalInput").ap()
    wr_in = nc.dram_tensor("wrq", [D, D], F32, kind="ExternalInput").ap()
    wi_in = nc.dram_tensor("wiq", [D, D], F32, kind="ExternalInput").ap()
    fb_t = nc.dram_tensor("fb_t", [128, 8], F32, kind="ExternalInput").ap()
    a_log = nc.dram_tensor("a_log", [8, 2], F32, kind="ExternalInput").ap()
    trc = nc.dram_tensor("trc", [D, D], F32, kind="ExternalInput").ap()
    trs = nc.dram_tensor("trs", [D, D], F32, kind="ExternalInput").ap()
    cons1 = nc.dram_tensor("cons1", [128, 672], F32, kind="ExternalInput").ap()
    out = nc.dram_tensor("out", [S, D], F32, kind="ExternalOutput").ap()

    st_xlnT = nc.dram_tensor("st_xlnT", [D, S], F32, kind="Internal").ap()
    st_xz = nc.dram_tensor("st_xz", [2 * D, S], F32, kind="Internal").ap()
    st_xin = nc.dram_tensor("st_xin", [D, S], F32, kind="Internal").ap()
    st_xp = nc.dram_tensor("st_xp", [112, S + 2 * TC], F32, kind="Internal").ap()
    st_sz = nc.dram_tensor("st_sz", [D, S], F32, kind="Internal").ap()
    st_dt = nc.dram_tensor("st_dt", [D, S + 2 * TC], F32, kind="Internal").ap()
    st_u = nc.dram_tensor("st_u", [D, S + 2 * TC], F32, kind="Internal").ap()
    y_part = nc.dram_tensor("y_part", [D, S], F32, kind="Internal").ap()
    y_cc = nc.dram_tensor("y_cc", [D, S], F32, kind="Internal").ap()
    st_mt = nc.dram_tensor("st_mt", [D, D], F32, kind="Internal").ap()
    dbg_outs = {}
    if dbg:
        for nm, shape in (("d_xz", [2048, 512]), ("d_xp", [112, 512]),
                          ("d_dt", [1024, 512]), ("d_u", [1024, 512]),
                          ("d_sz", [1024, 512]), ("d_y", [1024, 512]),
                          ("d_ycc", [1024, 512]), ("d_mt", [1024, 512]),
                          ("d_xln", [1024, 512])):
            dbg_outs[nm] = nc.dram_tensor(nm, shape, F32, kind="ExternalOutput").ap()
        dbg_outs["d_Htr"] = nc.dram_tensor("d_Htr", [128, 128 * 130], F32,
                                           kind="ExternalOutput").ap()
        for nm2, w in (("H", 128), ("tsq", 64), ("crci", 2), ("EXPC", 192),
                       ("s3", 64), ("e2", 64), ("LNB", 192), ("LNBo", 192),
                       ("qcmt", 64), ("s2", 64), ("pw", 64), ("lg", 64),
                       ("LNA", 66), ("LNAo", 66), ("dotv", 1), ("rn", 1),
                       ("pyt0", 64), ("pyt1", 64)):
            dbg_outs["ds_" + nm2] = nc.dram_tensor(
                "ds_" + nm2, [128, w], F32, kind="ExternalOutput").ap()

    t = {}
    ctxs = []

    def sb(name, shape, dtype=F32):
        hx = nc.sbuf_tensor(name, shape, dtype)
        t[name] = hx.__enter__()
        ctxs.append(hx)

    def psal(name, shape):
        hx = nc.psum_tensor(name, shape, F32)
        t[name] = hx.__enter__()
        ctxs.append(hx)

    sb("aF", [128, 16384])
    sb("a2F", [128, 4096])
    sb("aB", [128, 16384])
    sb("xlT_bf", [128, 4096])
        
    sb("cons1t", [128, 672])
    sb("anegv", [128, 2])
    sb("alog", [8, 2])
    sb("wxe_bf", [128, 8 * 112])
    sb("wdt_bf", [64, 1024])
    sb("bdt_sb", [128, 8])
    sb("bout_sb", [128, 8])
    sb("fb_sb", [128, 8])
    sb("mvt", [128, 4])
    sb("rstd", [128, 1])
    sb("trw0", [128, 128])
    sb("trw1", [128, 128])
    
    sb("H", [128, 128]); sb("tsq", [128, 64]); sb("crci", [128, 2])
    sb("hcs", [128, 2]); sb("scr2", [128, 2]); sb("scr2b", [128, 2])
    sb("dotv", [128, 1]); sb("rn", [128, 1]); sb("cg1", [128, 1])
    sb("gnn", [128, 1]); sb("gn", [128, 1]); sb("gp1", [128, 1])
    sb("LNA", [128, 66]); sb("LNAo", [128, 66]); sb("lg", [128, 64])
    sb("ag1", [128, 64]); sb("Xt", [128, 64]); sb("pw", [128, 64])
    sb("s2", [128, 64]); sb("H2", [128, 128]); sb("H3", [128, 128])
    sb("SQ", [128, 128]); sb("qcmt", [128, 64]); sb("lA", [128, 64])
    sb("cmA", [128, 64]); sb("e2", [128, 64]); sb("sD", [128, 64])
    sb("qD", [128, 64]); sb("LNB", [128, 192]); sb("LNBo", [128, 192])
    sb("EXC", [128, 192]); sb("EXPC", [128, 192]); sb("s3", [128, 64])
    sb("tvt", [128, 64]); sb("py0", [128, 64]); sb("jnkp", [128, 64])
    sb("pyt0", [128, 64]); sb("pyt1", [128, 64])
    sb("pxa", [128, 64]); sb("pxb", [128, 64]); sb("pxc", [128, 64])
    for pr in ("A", "B"):
        sb(f"Brv{pr}", [128, TC]); sb(f"Biv{pr}", [128, TC])
        sb(f"Crv{pr}", [128, TC]); sb(f"Civ{pr}", [128, TC])
        sb(f"XRr{pr}", [128, TC]); sb(f"XRi{pr}", [128, TC])
        sb(f"XR2{pr}", [128, 2 * TC])
        sb(f"mrx{pr}", [128, TC])
        sb(f"mrxp{pr}", [128, TC])
    sb("ySB0", [16, 4096])

    psal("ps_hc", [128, 2]); psal("ps_y0", [16, 512]); psal("ps_y1", [16, 512])
    psal("ps_mm0", [128, 512]); psal("ps_mm1", [128, 512]); psal("ps_xp", [112, 512])

    aF = t["aF"]
    a2F = t["a2F"]
    aB = t["aB"]

    big0 = aF[:, 0:4096]
    big1 = aF[:, 4096:8192]
    xt0 = aF[:, 8192:9216]
    xc0 = aF[:, 9216:10240]
    xlnf = aF[:, 10240:11264]
    sqt = aF[:, 11264:12288]
    evac = aF[:, 12288:12800]
    evb = aF[:, 12288:13312]
    silp3 = aF[:, 8192:12288]
    silf = aF[:, 12800:13312]
    xp_ev = aF[0:112, 13312:13824]
    xpblk = aF[0:64, 13824:14336]
    dtt = aF[:, 14336:14848]
    xint = aF[:, 14848:15360]
    ut = aF[:, 15360:15872]

    uRv = {"A": aF[:, 0:4096], "B": aF[:, 4096:8192]}
    alph = {"A": aF[:, 8192:12288], "B": aF[:, 12288:16384]}

    EtS = lambda kc: aF[:, kc * 1024:(kc + 1) * 1024]
    FtS = lambda kc: aF[:, 8192 + kc * 1024:8192 + (kc + 1) * 1024]
    wrk = a2F[:, 0:1024]; wik = a2F[:, 1024:2048]
    ck = a2F[:, 2048:3072]; sk = a2F[:, 3072:4096]

    att8 = aF[:, 0:4096]
    xmid = aF[:, 4096:8192]
    ff8 = aF[:, 8192:12288]
    ygk = aF[:, 12288:12800]
    szk = aF[:, 12800:13312]
    xt3 = aF[:, 13312:14336]
    xc3 = aF[:, 14336:15360]
    xl3 = aF[:, 15360:16384]

    win_bf = aB
    wout_bf = aB[:, 0:8192]
    mt_bf = aB[:, 8192:16384]
    xl2b = t["xlT_bf"]

    ps_tr = t["ps_mm1"][:, 0:128]

    with (
        nc.semaphore("g") as g,
        nc.semaphore("svA") as svA, nc.semaphore("svB") as svB,
        nc.semaphore("ssA") as ssA, nc.semaphore("ssB") as ssB,
        nc.semaphore("spe") as spe,
        nc.semaphore("pg") as pg,
        nc.semaphore("sye") as sye,
        nc.semaphore("sdi") as sdi,
        nc.semaphore("sdy") as sdy,
        nc.semaphore("g2") as g2,
        nc.Block() as block,
    ):
        c1t = t["cons1t"]
        Wmean = c1t[:, 0:128]
        W4 = c1t[:, 128:144]
        epsv01 = c1t[:, 144:146]
        ident = c1t[:, 406:534]
        cons_v = {"CM025": c1t[:, 664:665], "CP125": c1t[:, 665:666],
                  "JS": c1t[:, 148:212], "JD": t["jnkp"][:]}
        nc.const_aps.register(F32, 1e-8, c1t[:, 146:147])
        nc.const_aps.register(F32, 1e-5, c1t[:, 147:148])

        sq = Seq()
        sq.on("sync", lambda e: e.dma_start(out=c1t[:], in_=cons1[:]), dma=True)
        sq.on("sync", lambda e: e.dma_start(out=t["bdt_sb"][:], in_=b_dt_t[:]), dma=True)
        sq.on("sync", lambda e: e.dma_start(out=t["bout_sb"][:], in_=b_out_t[:]), dma=True)
        sq.on("sync", lambda e: e.dma_start(out=t["fb_sb"][:], in_=fb_t[:]), dma=True)
        sq.on("sync", lambda e: e.dma_start(out=t["alog"][:], in_=a_log[:]), dma=True)
        for kc in range(8):
            sq.on("gpsimd", lambda e, kc=kc: e.dma_start(
                out=win_bf[:, ts(kc, 2048)], in_=w_in[ts(kc, 128), :]), dma=True)
            sq.on("gpsimd", lambda e, kc=kc: e.dma_start(
                out=t["wxe_bf"][:, ts(kc, 112)], in_=w_xe[ts(kc, 128), :]), dma=True)
        sq.on("gpsimd", lambda e: e.dma_start(out=t["wdt_bf"][:], in_=w_dt[:]), dma=True)
        sq.on("gpsimd", lambda e: e.dma_start(
            out=t["anegv"][:],
            in_=a_log.rearrange("r (o c) -> r o c", o=1).broadcast_to([8, 16, 2])),
            dma=True)
        sq.on("scalar", lambda e: e.activation(out=t["anegv"][:], in_=t["anegv"][:],
                                               func=AF.Exp))
        sq.on("vector", lambda e: e.tensor_scalar(t["anegv"][:], t["anegv"][:], -1.0,
                                                  None, ALU.mult))
        sq.on("vector", lambda e: e.tensor_copy(t["H"][:, 0:64], c1t[:, 148:212]))
        sq.on("vector", lambda e: e.tensor_copy(t["H"][:, 64:128], c1t[:, 212:276]))
        sq.on("vector", lambda e: e.tensor_copy(t["tsq"][:], c1t[:, 276:340]))
        sq.on("vector", lambda e: e.tensor_copy(t["crci"][:], c1t[:, 340:342]))
        sq.on("vector", lambda e: e.tensor_copy(t["EXPC"][:, 64:128], c1t[:, 342:406]))

        for i in range(64):
            sq.on("sync", lambda e, i=i: e.dma_start(
                out=xt0, in_=x_in[ts(i, 128), :]), dma=True)
            sq.on("vector", lambda e: e.reduce_sum(t["mvt"][:, 0:1], xt0, axis=AXX))
            sq.on("vector", lambda e: e.tensor_scalar(t["mvt"][:, 1:2], t["mvt"][:, 0:1],
                                                      INV_D, None, ALU.mult))
            sq.on("vector", lambda e: e.tensor_scalar(xc0, xt0, t["mvt"][:, 1:2],
                                                      None, ALU.subtract))
            sq.on("scalar", lambda e: e.activation(out=sqt, in_=xc0, func=AF.Square))
            sq.on("vector", lambda e: e.reduce_sum(t["mvt"][:, 2:3], sqt, axis=AXX))
            sq.on("vector", lambda e: e.tensor_scalar(t["mvt"][:, 3:4], t["mvt"][:, 2:3],
                                                      INV_D, None, ALU.mult))
            sq.on("scalar", lambda e: e.activation(out=t["rstd"][:], in_=t["mvt"][:, 3:4],
                                                   func=AF.Ln, bias=1e-5))
            sq.on("scalar", lambda e: e.activation(out=t["rstd"][:], in_=t["rstd"][:],
                                                   func=AF.Exp, scale=-0.5))
            sq.on("vector", lambda e: e.tensor_scalar(xlnf, xc0, t["rstd"][:, 0:1],
                                                      None, ALU.mult))
            for kc in range(8):
                sq.on("tensor", lambda e, kc=kc: e.transpose(
                    ps_tr, xlnf[:, ts(kc, 128)], ident))
                sq.on("vector", lambda e, kc=kc: e.tensor_copy(
                    evb[:, ts(kc, 128)], ps_tr))
            sq.on("sync", lambda e, i=i: e.dma_start(
                out=st_xlnT.rearrange("(kc p) s -> p kc s", p=128)[:, :, ts(i, 128)],
                in_=evb.rearrange("p (kc c) -> p kc c", kc=8)), dma=True)

        for blk in range(16):
            sq.on("sync", lambda e, blk=blk: e.dma_start(
                out=t["xlT_bf"][:].rearrange("p (kc c) -> p kc c", kc=8),
                in_=st_xlnT.rearrange("(kc p) s -> p kc s", p=128)[:, :, ts(blk, 512)]),
                dma=True)
            for mc in range(16):
                def mmfn(e, mc=mc):
                    for kc in range(8):
                        inst = e.matmul(
                            t["ps_mm0"][:],
                            win_bf[:, kc * 2048 + mc * 128:kc * 2048 + (mc + 1) * 128],
                            t["xlT_bf"][:, ts(kc, 512)],
                            start=(kc == 0), stop=(kc == 7))
                    return inst
                sq.on("tensor", mmfn)
                sq.on("vector", lambda e: e.tensor_copy(evac, t["ps_mm0"][:]))
                sq.on("sync", lambda e, blk=blk, mc=mc: e.dma_start(
                    out=st_xz[ts(mc, 128), ts(blk, 512)], in_=evac), dma=True)

        for blk in range(16):
            sq.on("sync", lambda e, blk=blk: e.dma_start(
                out=big0.rearrange("p (kc c) -> p kc c", kc=8),
                in_=st_xz.rearrange("(kc p) s -> p kc s", p=128)[:, 0:8, ts(blk, 512)]),
                dma=True)
            for kc in range(8):
                sq.on("scalar", lambda e, kc=kc: e.activation(
                    out=big0[:, ts(kc, 512)], in_=big0[:, ts(kc, 512)],
                    func=AF.Silu))
                sq.on("sync", lambda e, blk=blk, kc=kc: e.dma_start(
                    out=st_xin[ts(kc, 128), ts(blk, 512)],
                    in_=big0[:, ts(kc, 512)]), dma=True)
            def xpfn(e):
                for kc in range(8):
                    inst = e.matmul(t["ps_xp"][:], t["wxe_bf"][:, ts(kc, 112)],
                                    big0[:, ts(kc, 512)],
                                    start=(kc == 0), stop=(kc == 7))
                return inst
            sq.on("tensor", xpfn)
            sq.on("vector", lambda e: e.tensor_copy(xp_ev, t["ps_xp"][:]))
            sq.on("sync", lambda e, blk=blk: e.dma_start(
                out=st_xp[:, ts(blk, 512)], in_=xp_ev), dma=True)
            sq.on("sync", lambda e, blk=blk: e.dma_start(
                out=big1.rearrange("p (kc c) -> p kc c", kc=8),
                in_=st_xz.rearrange("(kc p) s -> p kc s", p=128)[:, 8:16, ts(blk, 512)]),
                dma=True)
            for kc in range(8):
                sq.on("scalar", lambda e, kc=kc: e.activation(
                    out=silf, in_=big1[:, ts(kc, 512)], func=AF.Silu))
                sq.on("sync", lambda e, blk=blk, kc=kc: e.dma_start(
                    out=st_sz[ts(kc, 128), ts(blk, 512)], in_=silf), dma=True)

        for blk in range(16):
            sq.on("gpsimd", lambda e, blk=blk: e.dma_start(
                out=xpblk, in_=st_xp[0:64, ts(blk, 512)]), dma=True)
            for mc in range(8):
                sq.on("tensor", lambda e, mc=mc: e.matmul(
                    t["ps_mm0"][:], t["wdt_bf"][:, ts(mc, 128)], xpblk,
                    start=True, stop=True))
                sq.on("scalar", lambda e, mc=mc: e.activation(
                    out=dtt, in_=t["ps_mm0"][:], func=AF.Exp,
                    bias=t["bdt_sb"][:, mc:mc + 1]))
                sq.on("vector", lambda e: e.tensor_scalar(
                    dtt, dtt, 1.0, None, ALU.add))
                sq.on("scalar", lambda e: e.activation(
                    out=dtt, in_=dtt, func=AF.Ln))
                sq.on("sync", lambda e, blk=blk, mc=mc: e.dma_start(
                    out=xint, in_=st_xin[ts(mc, 128), ts(blk, 512)]), dma=True)
                sq.on("vector", lambda e: e.tensor_tensor(ut, dtt, xint, ALU.mult))
                sq.on("sync", lambda e, blk=blk, mc=mc: e.dma_start(
                    out=st_dt[ts(mc, 128), ts(blk, 512)], in_=dtt), dma=True)
                sq.on("sync", lambda e, blk=blk, mc=mc: e.dma_start(
                    out=st_u[ts(mc, 128), ts(blk, 512)], in_=ut), dma=True)
        sq.on("vector", lambda e: e.memset(ut, 0.0))
        for mc in range(8):
            sq.on("sync", lambda e, mc=mc: e.dma_start(
                out=st_u[ts(mc, 128), S:S + 2 * TC], in_=ut[:, 0:128]), dma=True)
            sq.on("sync", lambda e, mc=mc: e.dma_start(
                out=st_dt[ts(mc, 128), S:S + 2 * TC], in_=ut[:, 0:128]), dma=True)
        sq.on("sync", lambda e: e.dma_start(
            out=st_xp[:, S:S + 2 * TC], in_=ut[0:112, 0:128]), dma=True)
        G_P1 = sq.pos

        def cbufs(pr):
            d = {k: t[f"{k}{pr}"] for k in
                 ("Brv", "Biv", "Crv", "Civ", "XRr", "XRi", "XR2", "mrx",
                  "mrxp")}
            d["uRv"] = uRv[pr]
            d["alph"] = alph[pr]
            return d

        def chunk_dmas(e, cb, che):
            e.dma_start(
                out=cb["uRv"],
                in_=st_u.rearrange("(dh dl) s -> dh dl s", dh=16)[
                    :, :, ds(che * TC, TC)].partition_broadcast(8)).then_inc(sdi, 16)
            e.dma_start(
                out=cb["alph"],
                in_=st_dt.rearrange("(dh dl) s -> dh dl s", dh=16)[
                    :, :, ds(che * TC, TC)].partition_broadcast(8)).then_inc(sdi, 16)
            for i, (nm, row) in enumerate((("Brv", 64), ("Biv", 65), ("Crv", 80),
                                           ("Civ", 81), ("XRr", 96), ("XRi", 97))):
                rend = (row // 16) * 16 + 16
                e.dma_start(
                    out=cb[nm][:],
                    in_=st_xp[row:rend:2, ds(che * TC, TC)].rearrange(
                        "r (o c) -> r o c", o=1).broadcast_to([8, 16, TC])
                ).then_inc(sdi, 16)

        def prep_ops(cb):
            return [
                ("scalar", lambda e: e.activation(out=cb["alph"], in_=cb["alph"],
                                                  func=AF.Exp,
                                                  scale=t["anegv"][:, 0:1])),
                ("vector", lambda e: e.tensor_scalar(cb["XRr"][:], cb["XRr"][:],
                                                     1e-10, None, ALU.add)),
                ("vector", lambda e: e.tensor_tensor(t["pxa"][:], cb["XRr"][:],
                                                     cb["XRr"][:], ALU.mult)),
                ("vector", lambda e: e.tensor_tensor(t["pxb"][:], cb["XRi"][:],
                                                     cb["XRi"][:], ALU.mult)),
                ("vector", lambda e: e.tensor_tensor(t["pxa"][:], t["pxa"][:],
                                                     t["pxb"][:], ALU.add)),
                ("scalar", lambda e: e.activation(out=t["pxc"][:], in_=t["pxa"][:],
                                                  func=AF.Ln)),
                ("scalar", lambda e: e.activation(out=t["pxc"][:], in_=t["pxc"][:],
                                                  func=AF.Exp, scale=-0.5)),
                ("vector", lambda e: e.tensor_scalar(cb["mrx"][:], t["pxc"][:],
                                                     -0.25, None, ALU.mult)),
                ("vector", lambda e: e.tensor_scalar(cb["mrxp"][:], t["pxc"][:],
                                                     0.25, None, ALU.mult)),
                ("vector", lambda e: e.tensor_copy(cb["XR2"][:, 0:2 * TC:2],
                                                   cb["XRr"][:])),
                ("vector", lambda e: e.tensor_copy(cb["XR2"][:, 1:2 * TC:2],
                                                   cb["XRi"][:])),
            ]

        def emit_prep(engname, eng, cb, base, dma_target, m_pg, m_sdi):
            first = True
            for i, (e_, fn) in enumerate(prep_ops(cb)):
                if e_ != engname:
                    continue
                if first:
                    m_sdi.wait(eng, dma_target)
                    first = False
                m_pg.wait(eng, base + i + 4)
                fn(eng).then_inc(pg, 1)

        sq2 = Seq()
        for kc in range(8):
            sq2.on("sync", lambda e, kc=kc: e.dma_start(
                out=wrk, in_=wr_in[ts(kc, 128), :]), dma=True)
            sq2.on("sync", lambda e, kc=kc: e.dma_start(
                out=wik, in_=wi_in[ts(kc, 128), :]), dma=True)
            sq2.on("sync", lambda e, kc=kc: e.dma_start(
                out=ck, in_=trc[ts(kc, 128), :]), dma=True)
            sq2.on("sync", lambda e, kc=kc: e.dma_start(
                out=sk, in_=trs[ts(kc, 128), :]), dma=True)
            for mc in range(8):
                for fc in range(2):
                    def etfn(e, mc=mc, fc=fc):
                        e.matmul(t["ps_mm0"][:], wrk[:, ts(mc, 128)],
                                 ck[:, ts(fc, 512)], start=True, stop=False)
                        return e.matmul(t["ps_mm0"][:], wik[:, ts(mc, 128)],
                                        sk[:, ts(fc, 512)], start=False, stop=True)
                    sq2.on("tensor", etfn)
                    if kc == 0:
                        sq2.on("vector", lambda e, mc=mc, fc=fc: e.tensor_copy(
                            EtS(mc)[:, ts(fc, 512)], t["ps_mm0"][:]))
                    else:
                        sq2.on("vector", lambda e, mc=mc, fc=fc: e.tensor_tensor(
                            EtS(mc)[:, ts(fc, 512)], EtS(mc)[:, ts(fc, 512)],
                            t["ps_mm0"][:], ALU.add))
            sq2.on("vector", lambda e: e.tensor_scalar(wik, wik, -1.0, None, ALU.mult))
            for mc in range(8):
                for fc in range(2):
                    def ftfn(e, mc=mc, fc=fc):
                        e.matmul(t["ps_mm0"][:], wrk[:, ts(mc, 128)],
                                 sk[:, ts(fc, 512)], start=True, stop=False)
                        return e.matmul(t["ps_mm0"][:], wik[:, ts(mc, 128)],
                                        ck[:, ts(fc, 512)], start=False, stop=True)
                    sq2.on("tensor", ftfn)
                    if kc == 0:
                        sq2.on("vector", lambda e, mc=mc, fc=fc: e.tensor_copy(
                            FtS(mc)[:, ts(fc, 512)], t["ps_mm0"][:]))
                    else:
                        sq2.on("vector", lambda e, mc=mc, fc=fc: e.tensor_tensor(
                            FtS(mc)[:, ts(fc, 512)], FtS(mc)[:, ts(fc, 512)],
                            t["ps_mm0"][:], ALU.add))
        for mc in range(8):
            for fc in range(2):
                for kc in range(8):
                    sq2.on("sync", lambda e, mc=mc, kc=kc: e.dma_start(
                        out=t["trw0"][:], in_=trc[ts(kc, 128), ts(mc, 128)]), dma=True)
                    sq2.on("sync", lambda e, mc=mc, kc=kc: e.dma_start(
                        out=t["trw1"][:], in_=trs[ts(kc, 128), ts(mc, 128)]), dma=True)
                    def mtfn(e, fc=fc, kc=kc):
                        e.matmul(t["ps_mm0"][:], t["trw0"][:], EtS(kc)[:, ts(fc, 512)],
                                 start=(kc == 0), stop=False)
                        return e.matmul(t["ps_mm0"][:], t["trw1"][:],
                                        FtS(kc)[:, ts(fc, 512)], start=False,
                                        stop=(kc == 7))
                    sq2.on("tensor", mtfn)
                sq2.on("scalar", lambda e: e.activation(
                    out=a2F[:, 0:512], in_=t["ps_mm0"][:], func=AF.Identity,
                    scale=INV_D))
                sq2.on("sync", lambda e, mc=mc, fc=fc: e.dma_start(
                    out=st_mt[ts(mc, 128), ts(fc, 512)], in_=a2F[:, 0:512]),
                    dma=True)
        for bi in range(8):
            for bj in range(8):
                sq2.on("sync", lambda e, bi=bi, bj=bj: e.dma_start(
                    out=t["trw0"][:], in_=st_mt[ts(bi, 128), ts(bj, 128)]), dma=True)
                sq2.on("tensor", lambda e: e.transpose(ps_tr, t["trw0"][:], ident))
                sq2.on("vector", lambda e, bi=bi, bj=bj: e.tensor_copy(
                    mt_bf[:, bj * 1024 + bi * 128:bj * 1024 + (bi + 1) * 128], ps_tr))
        sq2.on("gpsimd", lambda e: e.collective_compute(
            "AllReduce", ALU.add, ins=[y_part[:]], outs=[y_cc[:]],
            replica_groups=[[0, 1], [2, 3], [4, 5], [6, 7]]))
        for kc in range(8):
            sq2.on("gpsimd", lambda e, kc=kc: e.dma_start(
                out=wout_bf[:, ts(kc, 1024)], in_=w_out[ts(kc, 128), :]), dma=True)
        for blk in range(16):
            for kc in range(8):
                sq2.on("sync", lambda e, blk=blk, kc=kc: e.dma_start(
                    out=ygk, in_=y_cc[ts(kc, 128), ts(blk, 512)]), dma=True)
                sq2.on("sync", lambda e, blk=blk, kc=kc: e.dma_start(
                    out=szk, in_=st_sz[ts(kc, 128), ts(blk, 512)]), dma=True)
                sq2.on("vector", lambda e, kc=kc: e.tensor_tensor(
                    silp3[:, ts(kc, 512)], ygk, szk, ALU.mult))
            for mc in range(8):
                def wofn(e, mc=mc):
                    for kc in range(8):
                        inst = e.matmul(
                            t["ps_mm0"][:],
                            wout_bf[:, kc * 1024 + mc * 128:kc * 1024 + (mc + 1) * 128],
                            silp3[:, ts(kc, 512)], start=(kc == 0), stop=(kc == 7))
                    return inst
                sq2.on("tensor", wofn)
                sq2.on("scalar", lambda e, mc=mc: e.activation(
                    out=att8[:, ts(mc, 512)], in_=t["ps_mm0"][:], func=AF.Identity,
                    bias=t["bout_sb"][:, mc:mc + 1]))
            for tk in range(4):
                sq2.on("sync", lambda e, blk=blk, tk=tk: e.dma_start(
                    out=xt3, in_=x_in[blk * 512 + tk * 128:blk * 512 + (tk + 1) * 128,
                                      :]), dma=True)
                for kc in range(8):
                    sq2.on("tensor", lambda e, tk=tk, kc=kc: e.transpose(
                        ps_tr, att8[:, kc * 512 + tk * 128:kc * 512 + (tk + 1) * 128], ident))
                    sq2.on("vector", lambda e, tk=tk, kc=kc: e.tensor_tensor(
                        xmid[:, tk * 1024 + kc * 128:tk * 1024 + (kc + 1) * 128], ps_tr,
                        xt3[:, ts(kc, 128)], ALU.add))
                xmv = xmid[:, ts(tk, 1024)]
                sq2.on("vector", lambda e, xmv=xmv: e.reduce_sum(
                    t["mvt"][:, 0:1], xmv, axis=AXX))
                sq2.on("vector", lambda e: e.tensor_scalar(
                    t["mvt"][:, 1:2], t["mvt"][:, 0:1], INV_D, None, ALU.mult))
                sq2.on("vector", lambda e, xmv=xmv: e.tensor_scalar(
                    xc3, xmv, t["mvt"][:, 1:2], None, ALU.subtract))
                sq2.on("scalar", lambda e: e.activation(out=xl3, in_=xc3,
                                                        func=AF.Square))
                sq2.on("vector", lambda e: e.reduce_sum(t["mvt"][:, 2:3], xl3,
                                                        axis=AXX))
                sq2.on("vector", lambda e: e.tensor_scalar(
                    t["mvt"][:, 3:4], t["mvt"][:, 2:3], INV_D, None, ALU.mult))
                sq2.on("scalar", lambda e: e.activation(
                    out=t["rstd"][:], in_=t["mvt"][:, 3:4], func=AF.Ln, bias=1e-5))
                sq2.on("scalar", lambda e: e.activation(
                    out=t["rstd"][:], in_=t["rstd"][:], func=AF.Exp, scale=-0.5))
                sq2.on("vector", lambda e: e.tensor_scalar(
                    xl3, xc3, t["rstd"][:, 0:1], None, ALU.mult))
                for kc in range(8):
                    sq2.on("tensor", lambda e, kc=kc: e.transpose(
                        ps_tr, xl3[:, kc * 128:(kc + 1) * 128], ident))
                    sq2.on("vector", lambda e, tk=tk, kc=kc: e.tensor_copy(
                        xl2b[:, kc * 512 + tk * 128:kc * 512 + (tk + 1) * 128], ps_tr))
            for mc in range(8):
                def fffn(e, mc=mc):
                    for kc in range(8):
                        inst = e.matmul(
                            t["ps_mm0"][:],
                            mt_bf[:, kc * 1024 + mc * 128:kc * 1024 + (mc + 1) * 128],
                            xl2b[:, ts(kc, 512)], start=(kc == 0), stop=(kc == 7))
                    return inst
                sq2.on("tensor", fffn)
                sq2.on("scalar", lambda e, mc=mc: e.activation(
                    out=ff8[:, ts(mc, 512)], in_=t["ps_mm0"][:], func=AF.Identity,
                    bias=t["fb_sb"][:, mc:mc + 1]))
            for tk in range(4):
                for kc in range(8):
                    sq2.on("tensor", lambda e, tk=tk, kc=kc: e.transpose(
                        ps_tr, ff8[:, kc * 512 + tk * 128:kc * 512 + (tk + 1) * 128], ident))
                    sq2.on("vector", lambda e, tk=tk, kc=kc: e.tensor_tensor(
                        xl3[:, kc * 128:(kc + 1) * 128], ps_tr,
                        xmid[:, tk * 1024 + kc * 128:tk * 1024 + (kc + 1) * 128], ALU.add))
                sq2.on("gpsimd", lambda e, blk=blk, tk=tk: e.dma_start(
                    out=out[blk * 512 + tk * 128:blk * 512 + (tk + 1) * 128, :],
                    in_=xl3), dma=True)
        G_POST = sq2.pos

        @block.sync
        def _(sync):
            sq.emit("sync", sync, g)
            sync.wait_ge(g, G_P1)
            chunk_dmas(sync, cbufs("A"), 0)
            chunk_dmas(sync, cbufs("B"), 1)
            mA = Mono(nc, sync, svA, 3, TC * V_INCS)
            mB = Mono(nc, sync, svB, 3, TC * V_INCS)
            with sync.Fori(0, npair) as cp:
                mA.wait(sync, TC * V_INCS + OFF_SV)
                chunk_dmas(sync, cbufs("A"), cp * 2 + 2)
                mB.wait(sync, TC * V_INCS + OFF_SV)
                chunk_dmas(sync, cbufs("B"), cp * 2 + 3)
            mA.check(); mB.check()
            sq2.emit("sync", sync, g2)
            sync.wait_ge(g2, G_POST)

        @block.vector
        def _(v):
            sq.emit("vector", v, g)
            v.wait_ge(g, G_P1)
            for sem, n in ((svA, OFF_SV), (svB, OFF_SV), (spe, 1), (pg, 4),
                           (sye, 1)):
                for _ in range(n):
                    v.memset(t["scr2"][:, 0:1], 0.0).then_inc(sem, 1)
            m_ssA = Mono(nc, v, ssA, 0, TC * S_INCS)
            m_ssB = Mono(nc, v, ssB, 0, TC * S_INCS)
            m_spe = Mono(nc, v, spe, 0, 2 * TC * PE_INCS)
            m_pg = Mono(nc, v, pg, 4, 2 * PREP_INCS)
            m_sdi = Mono(nc, v, sdi, 0, 2 * DMAS_PER_CHUNK * 16)
            with v.Fori(0, npair) as cp:
                for par in (0, 1):
                    sv = svA if par == 0 else svB
                    cb = cbufs("A" if par == 0 else "B")
                    ch = par
                    emit_prep("vector", v, cb, ch * PREP_INCS,
                              (ch + 1) * DMAS_PER_CHUNK * 16, m_pg, m_sdi)
                    m_ss = m_ssA if par == 0 else m_ssB
                    for k in range(TC):
                        emit_step_v(v, t, cons_v, spe, step_bulk(cb, k), sv, k,
                                    t["pyt0"] if k % 2 == 0 else t["pyt1"],
                                    ch, m_ss, m_spe, m_pg)
            for m in (m_ssA, m_ssB, m_spe, m_pg, m_sdi):
                m.check()
            v.wait_ge(sdi, (2 * npair + 2) * DMAS_PER_CHUNK * 16)
            v.wait_ge(pg, 2 * npair * PREP_INCS + 4)
            sq2.emit("vector", v, g2)
            v.wait_ge(g2, G_POST)

        @block.scalar
        def _(s):
            sq.emit("scalar", s, g)
            s.wait_ge(g, G_P1)
            m_svA = Mono(nc, s, svA, 0, TC * V_INCS)
            m_svB = Mono(nc, s, svB, 0, TC * V_INCS)
            m_speS = Mono(nc, s, spe, 2, 2 * TC * PE_INCS)
            m_sdyS = Mono(nc, s, sdy, 16, 32)
            m_pgS = Mono(nc, s, pg, 0, 2 * PREP_INCS)
            m_sdiS = Mono(nc, s, sdi, 0, 2 * DMAS_PER_CHUNK * 16)
            with s.Fori(0, npair) as cp:
                for par in (0, 1):
                    ss = ssA if par == 0 else ssB
                    ch = par
                    emit_prep("scalar", s, cbufs("A" if par == 0 else "B"),
                              ch * PREP_INCS, (ch + 1) * DMAS_PER_CHUNK * 16,
                              m_pgS, m_sdiS)
                    m_sv = m_svA if par == 0 else m_svB
                    ysb = t["ySB0"]
                    for k in range(TC):
                        emit_step_s(s, t, ss, k, m_sv)
                        if k % 8 == 7:
                            s8 = k // 8
                            m_speS.wait(s, ch * TC * PE_INCS
                                        + (k + 1) * PE_INCS + 2)
                            if s8 == 0:
                                m_sdyS.wait(s, 16 * ch + 32)
                            s.activation(
                                out=ysb[:].rearrange("dh (dl t) -> dh dl t",
                                                     dl=64)[:, :,
                                                            s8 * 8:(s8 + 1) * 8],
                                in_=t["ps_y" + str(s8 % 2)][:].rearrange(
                                    "dh (t dl) -> dh dl t", t=8),
                                func=AF.Identity).then_inc(sye, 1)
                m_pgS.wait(s, 2 * PREP_INCS)
            for m in (m_svA, m_svB, m_speS, m_sdyS, m_pgS, m_sdiS):
                m.check()
            sq2.emit("scalar", s, g2)
            s.wait_ge(g2, G_POST)

        @block.tensor
        def _(pe):
            sq.emit("tensor", pe, g)
            pe.wait_ge(g, G_P1)
            m_svAp = Mono(nc, pe, svA, 2, TC * V_INCS)
            m_svBp = Mono(nc, pe, svB, 2, TC * V_INCS)
            m_syeP = Mono(nc, pe, sye, 0, 2 * 8)
            pe.matmul(t["ps_hc"][:, 0:2], Wmean, t["crci"][:],
                      start=True, stop=False)
            pe.matmul(t["ps_hc"][:, 0:2], c1t[0:1, 534:662],
                      c1t[0:1, 662:664], start=False,
                      stop=True).then_inc(spe, 1)
            with pe.Fori(0, npair) as cp:
                for par in (0, 1):
                    ch = par
                    m_sv = m_svAp if par == 0 else m_svBp
                    for k in range(TC):
                        gslot = ch * 8 + k // 8
                        pyt = t["pyt0"] if k % 2 == 0 else t["pyt1"]
                        psy = t["ps_y0"] if (k // 8) % 2 == 0 else t["ps_y1"]
                        m_sv.wait(pe, k * V_INCS + 7 + OFF_SV)
                        pe.matmul(t["ps_hc"][:, 0:2], Wmean, t["crci"][:],
                                  start=True, stop=False)
                        pe.matmul(t["ps_hc"][:, 0:2], c1t[0:1, 534:662],
                                  c1t[0:1, 662:664], start=False,
                                  stop=True).then_inc(spe, 1)
                        m_sv.wait(pe, k * V_INCS + 8 + OFF_SV)
                        m_syeP.wait(pe, gslot + 1)
                        pe.matmul(psy[:, ts(k % 8, 64)], W4, pyt[:],
                                  start=(k % 8 == 0), stop=(k % 8 == 7)
                                  ).then_inc(spe, 1)
            for m in (m_svAp, m_svBp, m_syeP):
                m.check()
            sq2.emit("tensor", pe, g2)
            pe.wait_ge(g2, G_POST)

        @block.gpsimd
        def _(gp):
            sq.emit("gpsimd", gp, g)
            gp.wait_ge(g, G_P1)
            gp.dma_start(out=t["mvt"][:, 0:2], in_=cons1[:, 144:146]).then_inc(sdy, 16)
            gp.dma_start(out=t["mvt"][:, 2:4], in_=cons1[:, 144:146]).then_inc(sdy, 16)
            m_syeG = Mono(nc, gp, sye, 1, 2 * 8)
            with gp.Fori(0, npair) as cp:
                for par in (0, 1):
                    ch = par
                    ysb = t["ySB0"]
                    m_syeG.wait(gp, (ch + 1) * 8 + 1)
                    gp.dma_start(
                        out=y_part.rearrange("(dh dl) s -> dh dl s", dh=16)[
                            :, :, ds((cp * 2 + ch) * TC, TC)],
                        in_=ysb[:]
                    ).then_inc(sdy, 16)
                    if dbg:
                        gp.dma_start(
                            out=dbg_outs["d_Htr"][:, ds((cp * 2 + ch) * 128, 128)],
                            in_=t["H"][:]).then_inc(sdi, 16)
            m_syeG.check()
            gp.wait_ge(sdy, 32 + 2 * npair * 16)
            sq2.emit("gpsimd", gp, g2)
            gp.wait_ge(g2, G_POST)
            if dbg:
                dsem = sdi
                nd = 0
                for nm, src_ap in (("d_xz", st_xz[:, 0:512]),
                                   ("d_xp", st_xp[:, 0:512]),
                                   ("d_dt", st_dt[:, 0:512]),
                                   ("d_u", st_u[:, 0:512]),
                                   ("d_sz", st_sz[:, 0:512]),
                                   ("d_y", y_part[:, 0:512]),
                                   ("d_ycc", y_cc[:, 0:512]),
                                   ("d_mt", st_mt[:, 0:512])):
                    gp.dma_start(out=dbg_outs[nm][:], in_=src_ap).then_inc(dsem, 16)
                    nd += 1
                gp.dma_start(out=dbg_outs["d_xln"][:],
                             in_=st_xlnT[:, 0:512]).then_inc(dsem, 16)
                nd += 1
                for nm2 in ("H", "tsq", "crci", "EXPC", "s3",
                            "e2", "LNB", "LNBo", "qcmt", "s2",
                            "pw", "lg", "LNA", "LNAo", "dotv",
                            "rn", "pyt0", "pyt1"):
                    gp.dma_start(out=dbg_outs["ds_" + nm2][:],
                                 in_=t[nm2][:]).then_inc(dsem, 16)
                    nd += 1
                gp.wait_ge(dsem, (NCHUNK + 2) * DMAS_PER_CHUNK * 16 + nd * 16)

    for hx in reversed(ctxs):
        hx.__exit__(None, None, None)
    return nc


_NC_CACHE = None


def kernel(**inputs):
    global _NC_CACHE
    if _NC_CACHE is None:
        _NC_CACHE = build_nc()
    in_maps = [make_inputs(c, inputs) for c in range(8)]
    res = run_bass_kernel_spmd(_NC_CACHE, in_maps, core_ids=list(range(8)))
    outp = np.zeros((4, S, D), np.float32)
    for b in range(4):
        outp[b] = res.results[2 * b]["out"]
    return outp


# version 04
# speedup vs baseline: 1.0273x; 1.0025x over previous
import numpy as np
import concourse.bass as bass
import concourse.mybir as mybir
from concourse.bass_utils import run_bass_kernel_spmd

F32 = mybir.dt.float32
BF16 = mybir.dt.bfloat16
AF = mybir.ActivationFunctionType
ALU = mybir.AluOpType
ts = bass.ts
ds = bass.ds
AXX = mybir.AxisListType.X

S, D, N, R = 8192, 1024, 16, 64
TC = 64
NCHUNK = S // TC
NPAIR = NCHUNK // 2

V_INCS = 9
S_INCS = 9
PE_INCS = 2
OFF_SV = 3
PREP_INCS = 11
DMAS_PER_CHUNK = 8

DC2 = 1.0 + 1e-8
DC1 = -2.0 + 2e-8
DC0 = 1.0 + 1e-8
DC1_OVER_DC2 = DC1 / DC2
INV_D = 1.0 / D



def host_constants(h):
    phases = np.linspace(0.0, 2.0 * np.pi * (1.0 - 1.0 / N), N)
    ph = phases[8 * h:8 * h + 8]
    c1 = np.zeros((128, 672), np.float64)
    for k in range(128):
        for m in range(128):
            if k // 16 == m // 16:
                c1[k, m] = INV_D
    for p in range(128):
        c1[p, 128 + (p % 16)] = 1.0
    c1[:, 144] = 1e-10
    c1[:, 146] = 1e-8
    c1[:, 147] = 1e-5
    r_of_p = np.arange(128) // 16
    h0r = 0.01 * np.cos(ph)[r_of_p]
    h0i = 0.01 * np.sin(ph)[r_of_p]
    c1[:, 148:212] = (h0r + 1e-10)[:, None]
    c1[:, 212:276] = h0i[:, None]
    tsq0 = h0r * h0r + h0i * h0i
    c1[:, 276:340] = tsq0[:, None]
    c1[:, 340] = 64.0 * h0r
    c1[:, 341] = 64.0 * h0i
    c1[:, 342:406] = np.sqrt(tsq0 + 1e-8)[:, None]
    c1[:, 406:534] = np.eye(128)
    c1[0, 534:662] = 1.0
    c1[0, 662] = 1e-10
    c1[:, 664] = -0.25
    c1[:, 665] = 1.25
    return np.ascontiguousarray(c1.astype(np.float32))


def host_trig():
    idx = np.arange(D, dtype=np.int64)
    angm = 2.0 * np.pi * ((np.outer(idx, idx) % D).astype(np.float64)) / D
    return (np.ascontiguousarray(np.cos(angm).astype(np.float32)),
            np.ascontiguousarray(np.sin(angm).astype(np.float32)))


_TRIG = None


def make_inputs(core, full):
    global _TRIG
    if _TRIG is None:
        _TRIG = host_trig()
    trigC, trigS = _TRIG
    b, h = core // 2, core % 2
    c1 = host_constants(h)
    Wx = np.asarray(full["L0_attn_W_xproj"])
    bcols = [64 + (8 * h + nl) * 2 + cc for nl in range(8) for cc in range(2)]
    ccols = [96 + (8 * h + nl) * 2 + cc for nl in range(8) for cc in range(2)]
    rcols = [(8 * h + nl) * 2 + cc for nl in range(8) for cc in range(2)]
    w_xe = np.concatenate([Wx[:, 0:64], Wx[:, bcols], Wx[:, ccols],
                           np.asarray(full["L0_attn_W_res"])[:, rcols]], axis=1)
    f32 = lambda a: np.ascontiguousarray(np.asarray(a, np.float32))
    return {
        "x": f32(np.asarray(full["L0_attn_x"])[b]),
        "w_in": f32(full["L0_attn_W_in"]),
        "w_xe": f32(w_xe),
        "w_dt": f32(full["L0_attn_W_dt"]),
        "b_dt_t": f32(np.asarray(full["L0_attn_b_dt"]).reshape(8, 128).T),
        "w_out": f32(full["L0_attn_W_out"]),
        "b_out_t": f32(np.asarray(full["L0_attn_b_out"]).reshape(8, 128).T),
        "wrq": f32(full["L0_ffn_w_freq_real"]),
        "wiq": f32(full["L0_ffn_w_freq_imag"]),
        "fb_t": f32(np.asarray(full["L0_ffn_bias"]).reshape(8, 128).T),
        "a_log": f32(np.repeat(np.asarray(full["L0_attn_A_log"],
                                          np.float32)[8 * h:8 * h + 8, None], 2, 1)),
        "trc": trigC, "trs": trigS, "cons1": c1,
    }



class Seq:
    def __init__(self):
        self.ops = []
        self.pos = 0

    def on(self, eng, fn, dma=False):
        inc = 16 if dma else 1
        self.ops.append((eng, fn, self.pos, inc))
        self.pos += inc

    def emit(self, engname, eng, gsem):
        for e, fn, pos, inc in self.ops:
            if e != engname:
                continue
            eng.wait_ge(gsem, pos)
            fn(eng).then_inc(gsem, inc)


class Mono:

    def __init__(self, nc, eng, sem, init, period):
        self.nc, self.sem = nc, sem
        self.regs = nc.alloc_registers(None, engines=[eng.engine])
        nc.regs_mov(self.regs, init)
        self.init = init
        self.period = period
        self.cur = init

    def wait(self, eng, target):
        d = target - self.cur
        assert d >= 0, (self.cur, target)
        if d > 0:
            self.nc.regs_alu(self.regs, self.regs, d, op=ALU.add)
            self.cur = target
        eng.wait_ge(self.sem, self.regs)

    def check(self):
        assert self.cur == self.init + self.period, (self.cur, self.init,
                                                     self.period)


def step_bulk(cb, k):
    a3 = cb["alph"].rearrange("p (dl tt) -> p dl tt", dl=64)
    u3 = cb["uRv"].rearrange("p (dl tt) -> p dl tt", dl=64)
    return {
        "xr2": cb["XR2"][:, 2 * k:2 * k + 2],
        "mrx": cb["mrx"][:, k:k + 1],
        "mrxp": cb["mrxp"][:, k:k + 1],
        "alph": a3[:, :, k:k + 1],
        "u": u3[:, :, k:k + 1],
        "Br": cb["Brv"][:, k:k + 1], "Bi": cb["Biv"][:, k:k + 1],
        "Cr": cb["Crv"][:, k:k + 1], "Ci": cb["Civ"][:, k:k + 1],
    }

def emit_step_v(v, t, cons, spe, bulk, sv, k, pyt, ch, m_ss, m_spe, m_pg):
    STT = v.scalar_tensor_tensor
    TT = v.tensor_tensor
    TS = v.tensor_scalar
    JUNK = lambda: TS(cons["JD"], cons["JS"], 1.0, None, ALU.mult)
    if k == 0:
        m_pg.wait(v, (ch + 1) * PREP_INCS + 4)
    m_spe.wait(v, ch * TC * PE_INCS + k * PE_INCS + 2)
    v.tensor_copy(t["hcs"][:], t["ps_hc"][:, 0:2])
    STT(t["LNA"][:, 0:64], t["H"][:, 0:64], 2e-10, t["tsq"][:], ALU.mult,
        ALU.add)
    STT(t["scr2"][:], bulk["xr2"], 1.0, t["ps_hc"][:, 0:2], ALU.mult,
        ALU.mult, accum_out=t["dotv"][:])
    STT(t["scr2b"][:], t["hcs"][:], 1.0, t["hcs"][:], ALU.mult,
        ALU.mult, accum_out=t["LNA"][:, 64:65]).then_inc(sv, 1)
    m_ss.wait(v, k * S_INCS + 3)
    TT(t["cg1"][:], t["dotv"][:], t["rn"][:], ALU.mult)
    JUNK()
    JUNK()
    STT(t["gn"][:], t["cg1"][:], bulk["mrxp"], cons["CM025"], ALU.mult,
        ALU.add)
    STT(t["gp1"][:], t["cg1"][:], bulk["mrx"], cons["CP125"], ALU.mult,
        ALU.add)
    JUNK()
    STT(t["ag1"][:], bulk["alph"], t["gn"][:, 0:1], t["lg"][:], ALU.mult,
        ALU.mult)
    STT(t["Xt"][:], t["lg"][:], t["gp1"][:, 0:1], t["ag1"][:], ALU.mult,
        ALU.add)
    STT(t["pw"][:], t["LNAo"][:, 0:64], -0.5, t["Xt"][:], ALU.mult,
        ALU.add).then_inc(sv, 1)
    m_ss.wait(v, k * S_INCS + 4)
    TT(t["H2"][:, 0:64], t["H"][:, 0:64], t["s2"][:], ALU.mult)
    TT(t["H2"][:, 64:128], t["H"][:, 64:128], t["s2"][:], ALU.mult)
    STT(t["H3"][:, 0:64], bulk["u"], bulk["Br"], t["H2"][:, 0:64],
        ALU.mult, ALU.add)
    STT(t["H3"][:, 64:128], bulk["u"], bulk["Bi"], t["H2"][:, 64:128],
        ALU.mult, ALU.add)
    TS(t["H3"][:, 0:64], t["H3"][:, 0:64], 1e-10, None,
       ALU.add).then_inc(sv, 1)
    TT(t["SQ"][:, 0:64], t["H3"][:, 0:64], t["H3"][:, 0:64], ALU.mult)
    TT(t["SQ"][:, 64:128], t["H3"][:, 64:128], t["H3"][:, 64:128], ALU.mult)
    STT(t["LNB"][:, 128:192], t["SQ"][:, 0:64], 1.0, t["SQ"][:, 64:128],
        ALU.mult, ALU.add)
    JUNK()
    STT(t["qcmt"][:], t["H3"][:, 0:64], -2e-10, t["LNB"][:, 128:192],
        ALU.mult, ALU.add).then_inc(sv, 1)
    m_ss.wait(v, k * S_INCS + 7)
    TS(t["sD"][:], t["e2"][:], DC1_OVER_DC2, None, ALU.add)
    TS(t["LNB"][:, 0:64], t["e2"][:], 1.0, None, ALU.add)
    TT(t["qD"][:], t["e2"][:], t["sD"][:], ALU.mult)
    JUNK()
    TS(t["LNB"][:, 64:128], t["qD"][:], DC2, DC0, ALU.mult,
       ALU.add).then_inc(sv, 1)
    m_ss.wait(v, k * S_INCS + 8)
    STT(t["EXC"][:, 0:64], t["LNBo"][:, 128:192], -0.5, t["LNBo"][:, 0:64],
        ALU.mult, ALU.subtract)
    STT(t["EXC"][:, 64:128], t["LNBo"][:, 64:128], 0.5, t["LNBo"][:, 0:64],
        ALU.mult, ALU.subtract)
    TS(t["EXC"][:, 128:192], t["LNBo"][:, 0:64], -1.0, None,
       ALU.mult).then_inc(sv, 1)
    m_ss.wait(v, k * S_INCS + 9)
    STT(t["s3"][:], t["e2"][:], -1.0, t["EXPC"][:, 0:64], ALU.add, ALU.mult)
    STT(t["H"][:, 0:64], t["H3"][:, 0:64], 1.0, t["s3"][:], ALU.mult,
        ALU.mult, accum_out=t["crci"][:, 0:1])
    STT(t["H"][:, 64:128], t["H3"][:, 64:128], 1.0, t["s3"][:], ALU.mult,
        ALU.mult, accum_out=t["crci"][:, 1:2]).then_inc(sv, 1)
    STT(t["tvt"][:], t["e2"][:], -1.0, t["EXPC"][:, 128:192], ALU.add,
        ALU.mult)
    TS(t["py0"][:], t["H"][:, 0:64], bulk["Cr"], None, ALU.mult)
    TS(t["H"][:, 0:64], t["H"][:, 0:64], 1e-10, None, ALU.add)
    STT(pyt[:], t["H"][:, 64:128], bulk["Ci"], t["py0"][:], ALU.mult,
        ALU.add).then_inc(sv, 1)
    TT(t["tsq"][:], t["tvt"][:], t["tvt"][:], ALU.mult).then_inc(sv, 1)

def emit_step_s(s, t, ss, k, m_sv):
    s.activation(out=t["lg"][:], in_=t["EXPC"][:, 64:128], func=AF.Ln,
                 bias=1e-8).then_inc(ss, 1)
    m_sv.wait(s, k * V_INCS + 1 + OFF_SV)
    s.activation(out=t["LNAo"][:, 0:65], in_=t["LNA"][:, 0:65],
                 func=AF.Ln).then_inc(ss, 1)
    s.activation(out=t["rn"][:], in_=t["LNAo"][:, 64:65], func=AF.Exp,
                 scale=-0.5).then_inc(ss, 1)
    m_sv.wait(s, k * V_INCS + 2 + OFF_SV)
    s.activation(out=t["s2"][:], in_=t["pw"][:], func=AF.Exp).then_inc(ss, 1)
    m_sv.wait(s, k * V_INCS + 4 + OFF_SV)
    s.activation(out=t["lA"][:], in_=t["qcmt"][:], func=AF.Ln,
                 bias=1e-8).then_inc(ss, 1)
    s.activation(out=t["cmA"][:], in_=t["lA"][:], func=AF.Exp,
                 scale=0.5).then_inc(ss, 1)
    s.activation(out=t["e2"][:], in_=t["cmA"][:], func=AF.Exp,
                 scale=2.0).then_inc(ss, 1)
    m_sv.wait(s, k * V_INCS + 5 + OFF_SV)
    s.activation(out=t["LNBo"][:], in_=t["LNB"][:],
                 func=AF.Ln).then_inc(ss, 1)
    m_sv.wait(s, k * V_INCS + 6 + OFF_SV)
    s.activation(out=t["EXPC"][:], in_=t["EXC"][:],
                 func=AF.Exp).then_inc(ss, 1)




def build_nc(dbg=False, npair=NPAIR):
    nc = bass.Bass("TRN2", target_bir_lowering=False, debug=False)

    x_in = nc.dram_tensor("x", [S, D], F32, kind="ExternalInput").ap()
    w_in = nc.dram_tensor("w_in", [D, 2 * D], F32, kind="ExternalInput").ap()
    w_xe = nc.dram_tensor("w_xe", [D, 112], F32, kind="ExternalInput").ap()
    w_dt = nc.dram_tensor("w_dt", [R, D], F32, kind="ExternalInput").ap()
    b_dt_t = nc.dram_tensor("b_dt_t", [128, 8], F32, kind="ExternalInput").ap()
    w_out = nc.dram_tensor("w_out", [D, D], F32, kind="ExternalInput").ap()
    b_out_t = nc.dram_tensor("b_out_t", [128, 8], F32, kind="ExternalInput").ap()
    wr_in = nc.dram_tensor("wrq", [D, D], F32, kind="ExternalInput").ap()
    wi_in = nc.dram_tensor("wiq", [D, D], F32, kind="ExternalInput").ap()
    fb_t = nc.dram_tensor("fb_t", [128, 8], F32, kind="ExternalInput").ap()
    a_log = nc.dram_tensor("a_log", [8, 2], F32, kind="ExternalInput").ap()
    trc = nc.dram_tensor("trc", [D, D], F32, kind="ExternalInput").ap()
    trs = nc.dram_tensor("trs", [D, D], F32, kind="ExternalInput").ap()
    cons1 = nc.dram_tensor("cons1", [128, 672], F32, kind="ExternalInput").ap()
    out = nc.dram_tensor("out", [S, D], F32, kind="ExternalOutput").ap()

    st_xlnT = nc.dram_tensor("st_xlnT", [D, S], F32, kind="Internal").ap()
    st_xz = nc.dram_tensor("st_xz", [2 * D, S], F32, kind="Internal").ap()
    st_xin = nc.dram_tensor("st_xin", [D, S], F32, kind="Internal").ap()
    st_xp = nc.dram_tensor("st_xp", [112, S + 2 * TC], F32, kind="Internal").ap()
    st_sz = nc.dram_tensor("st_sz", [D, S], F32, kind="Internal").ap()
    st_dt = nc.dram_tensor("st_dt", [D, S + 2 * TC], F32, kind="Internal").ap()
    st_u = nc.dram_tensor("st_u", [D, S + 2 * TC], F32, kind="Internal").ap()
    y_part = nc.dram_tensor("y_part", [D, S], F32, kind="Internal").ap()
    y_cc = nc.dram_tensor("y_cc", [D, S], F32, kind="Internal").ap()
    st_mt = nc.dram_tensor("st_mt", [D, D], F32, kind="Internal").ap()
    dbg_outs = {}
    if dbg:
        for nm, shape in (("d_xz", [2048, 512]), ("d_xp", [112, 512]),
                          ("d_dt", [1024, 512]), ("d_u", [1024, 512]),
                          ("d_sz", [1024, 512]), ("d_y", [1024, 512]),
                          ("d_ycc", [1024, 512]), ("d_mt", [1024, 512]),
                          ("d_xln", [1024, 512])):
            dbg_outs[nm] = nc.dram_tensor(nm, shape, F32, kind="ExternalOutput").ap()
        dbg_outs["d_Htr"] = nc.dram_tensor("d_Htr", [128, 128 * 130], F32,
                                           kind="ExternalOutput").ap()
        for nm2, w in (("H", 128), ("tsq", 64), ("crci", 2), ("EXPC", 192),
                       ("s3", 64), ("e2", 64), ("LNB", 192), ("LNBo", 192),
                       ("qcmt", 64), ("s2", 64), ("pw", 64), ("lg", 64),
                       ("LNA", 66), ("LNAo", 66), ("dotv", 1), ("rn", 1),
                       ("pyt0", 64), ("pyt1", 64)):
            dbg_outs["ds_" + nm2] = nc.dram_tensor(
                "ds_" + nm2, [128, w], F32, kind="ExternalOutput").ap()

    t = {}
    ctxs = []

    def sb(name, shape, dtype=F32):
        hx = nc.sbuf_tensor(name, shape, dtype)
        t[name] = hx.__enter__()
        ctxs.append(hx)

    def psal(name, shape):
        hx = nc.psum_tensor(name, shape, F32)
        t[name] = hx.__enter__()
        ctxs.append(hx)

    sb("aF", [128, 16384])
    sb("a2F", [128, 4096])
    sb("aB", [128, 16384])
    sb("xlT_bf", [128, 4096])
        
    sb("cons1t", [128, 672])
    sb("anegv", [128, 2])
    sb("alog", [8, 2])
    sb("wxe_bf", [128, 8 * 112])
    sb("wdt_bf", [64, 1024])
    sb("bdt_sb", [128, 8])
    sb("bout_sb", [128, 8])
    sb("fb_sb", [128, 8])
    sb("mvt", [128, 4])
    sb("rstd", [128, 1])
    sb("trw0", [128, 128])
    sb("trw1", [128, 128])
    
    sb("H", [128, 128]); sb("tsq", [128, 64]); sb("crci", [128, 2])
    sb("hcs", [128, 2]); sb("scr2", [128, 2]); sb("scr2b", [128, 2])
    sb("dotv", [128, 1]); sb("rn", [128, 1]); sb("cg1", [128, 1])
    sb("gnn", [128, 1]); sb("gn", [128, 1]); sb("gp1", [128, 1])
    sb("LNA", [128, 66]); sb("LNAo", [128, 66]); sb("lg", [128, 64])
    sb("ag1", [128, 64]); sb("Xt", [128, 64]); sb("pw", [128, 64])
    sb("s2", [128, 64]); sb("H2", [128, 128]); sb("H3", [128, 128])
    sb("SQ", [128, 128]); sb("qcmt", [128, 64]); sb("lA", [128, 64])
    sb("cmA", [128, 64]); sb("e2", [128, 64]); sb("sD", [128, 64])
    sb("qD", [128, 64]); sb("LNB", [128, 192]); sb("LNBo", [128, 192])
    sb("EXC", [128, 192]); sb("EXPC", [128, 192]); sb("s3", [128, 64])
    sb("tvt", [128, 64]); sb("py0", [128, 64]); sb("jnkp", [128, 64])
    sb("pyt0", [128, 64]); sb("pyt1", [128, 64])
    sb("pxa", [128, 64]); sb("pxb", [128, 64]); sb("pxc", [128, 64])
    for pr in ("A", "B"):
        sb(f"Brv{pr}", [128, TC]); sb(f"Biv{pr}", [128, TC])
        sb(f"Crv{pr}", [128, TC]); sb(f"Civ{pr}", [128, TC])
        sb(f"XRr{pr}", [128, TC]); sb(f"XRi{pr}", [128, TC])
        sb(f"XR2{pr}", [128, 2 * TC])
        sb(f"mrx{pr}", [128, TC])
        sb(f"mrxp{pr}", [128, TC])
    sb("ySB0", [16, 4096])

    psal("ps_hc", [128, 2]); psal("ps_y0", [16, 512]); psal("ps_y1", [16, 512])
    psal("ps_mm0", [128, 512]); psal("ps_mm1", [128, 512]); psal("ps_xp", [112, 512])

    aF = t["aF"]
    a2F = t["a2F"]
    aB = t["aB"]

    big0 = aF[:, 0:4096]
    big1 = aF[:, 4096:8192]
    xt0 = aF[:, 8192:9216]
    xc0 = aF[:, 9216:10240]
    xlnf = aF[:, 10240:11264]
    sqt = aF[:, 11264:12288]
    evac = aF[:, 12288:12800]
    evb = aF[:, 12288:13312]
    silp3 = aF[:, 8192:12288]
    silf = aF[:, 12800:13312]
    xp_ev = aF[0:112, 13312:13824]
    xpblk = aF[0:64, 13824:14336]
    dtt = aF[:, 14336:14848]
    xint = aF[:, 14848:15360]
    ut = aF[:, 15360:15872]

    uRv = {"A": aF[:, 0:4096], "B": aF[:, 4096:8192]}
    alph = {"A": aF[:, 8192:12288], "B": aF[:, 12288:16384]}

    EtS = lambda kc: aF[:, kc * 1024:(kc + 1) * 1024]
    FtS = lambda kc: aF[:, 8192 + kc * 1024:8192 + (kc + 1) * 1024]
    wrk = a2F[:, 0:1024]; wik = a2F[:, 1024:2048]
    ck = a2F[:, 2048:3072]; sk = a2F[:, 3072:4096]

    att8 = aF[:, 0:4096]
    xmid = aF[:, 4096:8192]
    ff8 = aF[:, 8192:12288]
    ygk = aF[:, 12288:12800]
    szk = aF[:, 12800:13312]
    xt3 = aF[:, 13312:14336]
    xc3 = aF[:, 14336:15360]
    xl3 = aF[:, 15360:16384]

    win_bf = aB
    wout_bf = aB[:, 0:8192]
    mt_bf = aB[:, 8192:16384]
    xl2b = t["xlT_bf"]

    ps_tr = t["ps_mm1"][:, 0:128]

    with (
        nc.semaphore("g") as g,
        nc.semaphore("svA") as svA, nc.semaphore("svB") as svB,
        nc.semaphore("ssA") as ssA, nc.semaphore("ssB") as ssB,
        nc.semaphore("spe") as spe,
        nc.semaphore("pg") as pg,
        nc.semaphore("sye") as sye,
        nc.semaphore("sdi") as sdi,
        nc.semaphore("sdy") as sdy,
        nc.semaphore("g2") as g2,
        nc.Block() as block,
    ):
        c1t = t["cons1t"]
        Wmean = c1t[:, 0:128]
        W4 = c1t[:, 128:144]
        epsv01 = c1t[:, 144:146]
        ident = c1t[:, 406:534]
        cons_v = {"CM025": c1t[:, 664:665], "CP125": c1t[:, 665:666],
                  "JS": c1t[:, 148:212], "JD": t["jnkp"][:]}
        nc.const_aps.register(F32, 1e-8, c1t[:, 146:147])
        nc.const_aps.register(F32, 1e-5, c1t[:, 147:148])

        sq = Seq()
        sq.on("sync", lambda e: e.dma_start(out=c1t[:], in_=cons1[:]), dma=True)
        sq.on("sync", lambda e: e.dma_start(out=t["bdt_sb"][:], in_=b_dt_t[:]), dma=True)
        sq.on("sync", lambda e: e.dma_start(out=t["bout_sb"][:], in_=b_out_t[:]), dma=True)
        sq.on("sync", lambda e: e.dma_start(out=t["fb_sb"][:], in_=fb_t[:]), dma=True)
        sq.on("sync", lambda e: e.dma_start(out=t["alog"][:], in_=a_log[:]), dma=True)
        for kc in range(8):
            sq.on("sync", lambda e, kc=kc: e.dma_start(
                out=win_bf[:, ts(kc, 2048)], in_=w_in[ts(kc, 128), :]), dma=True)
            sq.on("sync", lambda e, kc=kc: e.dma_start(
                out=t["wxe_bf"][:, ts(kc, 112)], in_=w_xe[ts(kc, 128), :]), dma=True)
        sq.on("sync", lambda e: e.dma_start(out=t["wdt_bf"][:], in_=w_dt[:]), dma=True)
        sq.on("sync", lambda e: e.dma_start(
            out=t["anegv"][:],
            in_=a_log.rearrange("r (o c) -> r o c", o=1).broadcast_to([8, 16, 2])),
            dma=True)
        sq.on("scalar", lambda e: e.activation(out=t["anegv"][:], in_=t["anegv"][:],
                                               func=AF.Exp))
        sq.on("vector", lambda e: e.tensor_scalar(t["anegv"][:], t["anegv"][:], -1.0,
                                                  None, ALU.mult))
        sq.on("vector", lambda e: e.tensor_copy(t["H"][:, 0:64], c1t[:, 148:212]))
        sq.on("vector", lambda e: e.tensor_copy(t["H"][:, 64:128], c1t[:, 212:276]))
        sq.on("vector", lambda e: e.tensor_copy(t["tsq"][:], c1t[:, 276:340]))
        sq.on("vector", lambda e: e.tensor_copy(t["crci"][:], c1t[:, 340:342]))
        sq.on("vector", lambda e: e.tensor_copy(t["EXPC"][:, 64:128], c1t[:, 342:406]))

        for i in range(64):
            sq.on("sync", lambda e, i=i: e.dma_start(
                out=xt0, in_=x_in[ts(i, 128), :]), dma=True)
            sq.on("vector", lambda e: e.reduce_sum(t["mvt"][:, 0:1], xt0, axis=AXX))
            sq.on("vector", lambda e: e.tensor_scalar(t["mvt"][:, 1:2], t["mvt"][:, 0:1],
                                                      INV_D, None, ALU.mult))
            sq.on("vector", lambda e: e.tensor_scalar(xc0, xt0, t["mvt"][:, 1:2],
                                                      None, ALU.subtract))
            sq.on("scalar", lambda e: e.activation(out=sqt, in_=xc0, func=AF.Square))
            sq.on("vector", lambda e: e.reduce_sum(t["mvt"][:, 2:3], sqt, axis=AXX))
            sq.on("vector", lambda e: e.tensor_scalar(t["mvt"][:, 3:4], t["mvt"][:, 2:3],
                                                      INV_D, None, ALU.mult))
            sq.on("scalar", lambda e: e.activation(out=t["rstd"][:], in_=t["mvt"][:, 3:4],
                                                   func=AF.Ln, bias=1e-5))
            sq.on("scalar", lambda e: e.activation(out=t["rstd"][:], in_=t["rstd"][:],
                                                   func=AF.Exp, scale=-0.5))
            sq.on("vector", lambda e: e.tensor_scalar(xlnf, xc0, t["rstd"][:, 0:1],
                                                      None, ALU.mult))
            for kc in range(8):
                sq.on("tensor", lambda e, kc=kc: e.transpose(
                    ps_tr, xlnf[:, ts(kc, 128)], ident))
                sq.on("vector", lambda e, kc=kc: e.tensor_copy(
                    evb[:, ts(kc, 128)], ps_tr))
            sq.on("sync", lambda e, i=i: e.dma_start(
                out=st_xlnT.rearrange("(kc p) s -> p kc s", p=128)[:, :, ts(i, 128)],
                in_=evb.rearrange("p (kc c) -> p kc c", kc=8)), dma=True)

        for blk in range(16):
            sq.on("sync", lambda e, blk=blk: e.dma_start(
                out=t["xlT_bf"][:].rearrange("p (kc c) -> p kc c", kc=8),
                in_=st_xlnT.rearrange("(kc p) s -> p kc s", p=128)[:, :, ts(blk, 512)]),
                dma=True)
            for mc in range(16):
                def mmfn(e, mc=mc):
                    for kc in range(8):
                        inst = e.matmul(
                            t["ps_mm0"][:],
                            win_bf[:, kc * 2048 + mc * 128:kc * 2048 + (mc + 1) * 128],
                            t["xlT_bf"][:, ts(kc, 512)],
                            start=(kc == 0), stop=(kc == 7))
                    return inst
                sq.on("tensor", mmfn)
                sq.on("vector", lambda e: e.tensor_copy(evac, t["ps_mm0"][:]))
                sq.on("sync", lambda e, blk=blk, mc=mc: e.dma_start(
                    out=st_xz[ts(mc, 128), ts(blk, 512)], in_=evac), dma=True)

        for blk in range(16):
            sq.on("sync", lambda e, blk=blk: e.dma_start(
                out=big0.rearrange("p (kc c) -> p kc c", kc=8),
                in_=st_xz.rearrange("(kc p) s -> p kc s", p=128)[:, 0:8, ts(blk, 512)]),
                dma=True)
            for kc in range(8):
                sq.on("scalar", lambda e, kc=kc: e.activation(
                    out=big0[:, ts(kc, 512)], in_=big0[:, ts(kc, 512)],
                    func=AF.Silu))
                sq.on("sync", lambda e, blk=blk, kc=kc: e.dma_start(
                    out=st_xin[ts(kc, 128), ts(blk, 512)],
                    in_=big0[:, ts(kc, 512)]), dma=True)
            def xpfn(e):
                for kc in range(8):
                    inst = e.matmul(t["ps_xp"][:], t["wxe_bf"][:, ts(kc, 112)],
                                    big0[:, ts(kc, 512)],
                                    start=(kc == 0), stop=(kc == 7))
                return inst
            sq.on("tensor", xpfn)
            sq.on("vector", lambda e: e.tensor_copy(xp_ev, t["ps_xp"][:]))
            sq.on("sync", lambda e, blk=blk: e.dma_start(
                out=st_xp[:, ts(blk, 512)], in_=xp_ev), dma=True)
            sq.on("sync", lambda e, blk=blk: e.dma_start(
                out=big1.rearrange("p (kc c) -> p kc c", kc=8),
                in_=st_xz.rearrange("(kc p) s -> p kc s", p=128)[:, 8:16, ts(blk, 512)]),
                dma=True)
            for kc in range(8):
                sq.on("scalar", lambda e, kc=kc: e.activation(
                    out=silf, in_=big1[:, ts(kc, 512)], func=AF.Silu))
                sq.on("sync", lambda e, blk=blk, kc=kc: e.dma_start(
                    out=st_sz[ts(kc, 128), ts(blk, 512)], in_=silf), dma=True)

        for blk in range(16):
            sq.on("sync", lambda e, blk=blk: e.dma_start(
                out=xpblk, in_=st_xp[0:64, ts(blk, 512)]), dma=True)
            for mc in range(8):
                sq.on("tensor", lambda e, mc=mc: e.matmul(
                    t["ps_mm0"][:], t["wdt_bf"][:, ts(mc, 128)], xpblk,
                    start=True, stop=True))
                sq.on("scalar", lambda e, mc=mc: e.activation(
                    out=dtt, in_=t["ps_mm0"][:], func=AF.Exp,
                    bias=t["bdt_sb"][:, mc:mc + 1]))
                sq.on("vector", lambda e: e.tensor_scalar(
                    dtt, dtt, 1.0, None, ALU.add))
                sq.on("scalar", lambda e: e.activation(
                    out=dtt, in_=dtt, func=AF.Ln))
                sq.on("sync", lambda e, blk=blk, mc=mc: e.dma_start(
                    out=xint, in_=st_xin[ts(mc, 128), ts(blk, 512)]), dma=True)
                sq.on("vector", lambda e: e.tensor_tensor(ut, dtt, xint, ALU.mult))
                sq.on("sync", lambda e, blk=blk, mc=mc: e.dma_start(
                    out=st_dt[ts(mc, 128), ts(blk, 512)], in_=dtt), dma=True)
                sq.on("sync", lambda e, blk=blk, mc=mc: e.dma_start(
                    out=st_u[ts(mc, 128), ts(blk, 512)], in_=ut), dma=True)
        sq.on("vector", lambda e: e.memset(ut, 0.0))
        for mc in range(8):
            sq.on("sync", lambda e, mc=mc: e.dma_start(
                out=st_u[ts(mc, 128), S:S + 2 * TC], in_=ut[:, 0:128]), dma=True)
            sq.on("sync", lambda e, mc=mc: e.dma_start(
                out=st_dt[ts(mc, 128), S:S + 2 * TC], in_=ut[:, 0:128]), dma=True)
        sq.on("sync", lambda e: e.dma_start(
            out=st_xp[:, S:S + 2 * TC], in_=ut[0:112, 0:128]), dma=True)
        G_P1 = sq.pos

        def cbufs(pr):
            d = {k: t[f"{k}{pr}"] for k in
                 ("Brv", "Biv", "Crv", "Civ", "XRr", "XRi", "XR2", "mrx",
                  "mrxp")}
            d["uRv"] = uRv[pr]
            d["alph"] = alph[pr]
            return d

        def chunk_dmas(e, cb, che):
            e.dma_start(
                out=cb["uRv"],
                in_=st_u.rearrange("(dh dl) s -> dh dl s", dh=16)[
                    :, :, ds(che * TC, TC)].partition_broadcast(8)).then_inc(sdi, 16)
            e.dma_start(
                out=cb["alph"],
                in_=st_dt.rearrange("(dh dl) s -> dh dl s", dh=16)[
                    :, :, ds(che * TC, TC)].partition_broadcast(8)).then_inc(sdi, 16)
            for i, (nm, row) in enumerate((("Brv", 64), ("Biv", 65), ("Crv", 80),
                                           ("Civ", 81), ("XRr", 96), ("XRi", 97))):
                rend = (row // 16) * 16 + 16
                e.dma_start(
                    out=cb[nm][:],
                    in_=st_xp[row:rend:2, ds(che * TC, TC)].rearrange(
                        "r (o c) -> r o c", o=1).broadcast_to([8, 16, TC])
                ).then_inc(sdi, 16)

        def prep_ops(cb):
            return [
                ("scalar", lambda e: e.activation(out=cb["alph"], in_=cb["alph"],
                                                  func=AF.Exp,
                                                  scale=t["anegv"][:, 0:1])),
                ("vector", lambda e: e.tensor_scalar(cb["XRr"][:], cb["XRr"][:],
                                                     1e-10, None, ALU.add)),
                ("vector", lambda e: e.tensor_tensor(t["pxa"][:], cb["XRr"][:],
                                                     cb["XRr"][:], ALU.mult)),
                ("vector", lambda e: e.tensor_tensor(t["pxb"][:], cb["XRi"][:],
                                                     cb["XRi"][:], ALU.mult)),
                ("vector", lambda e: e.tensor_tensor(t["pxa"][:], t["pxa"][:],
                                                     t["pxb"][:], ALU.add)),
                ("scalar", lambda e: e.activation(out=t["pxc"][:], in_=t["pxa"][:],
                                                  func=AF.Ln)),
                ("scalar", lambda e: e.activation(out=t["pxc"][:], in_=t["pxc"][:],
                                                  func=AF.Exp, scale=-0.5)),
                ("vector", lambda e: e.tensor_scalar(cb["mrx"][:], t["pxc"][:],
                                                     -0.25, None, ALU.mult)),
                ("vector", lambda e: e.tensor_scalar(cb["mrxp"][:], t["pxc"][:],
                                                     0.25, None, ALU.mult)),
                ("vector", lambda e: e.tensor_copy(cb["XR2"][:, 0:2 * TC:2],
                                                   cb["XRr"][:])),
                ("vector", lambda e: e.tensor_copy(cb["XR2"][:, 1:2 * TC:2],
                                                   cb["XRi"][:])),
            ]

        def emit_prep(engname, eng, cb, base, dma_target, m_pg, m_sdi):
            first = True
            for i, (e_, fn) in enumerate(prep_ops(cb)):
                if e_ != engname:
                    continue
                if first:
                    m_sdi.wait(eng, dma_target)
                    first = False
                m_pg.wait(eng, base + i + 4)
                fn(eng).then_inc(pg, 1)

        sq2 = Seq()
        for kc in range(8):
            sq2.on("sync", lambda e, kc=kc: e.dma_start(
                out=wrk, in_=wr_in[ts(kc, 128), :]), dma=True)
            sq2.on("sync", lambda e, kc=kc: e.dma_start(
                out=wik, in_=wi_in[ts(kc, 128), :]), dma=True)
            sq2.on("sync", lambda e, kc=kc: e.dma_start(
                out=ck, in_=trc[ts(kc, 128), :]), dma=True)
            sq2.on("sync", lambda e, kc=kc: e.dma_start(
                out=sk, in_=trs[ts(kc, 128), :]), dma=True)
            for mc in range(8):
                for fc in range(2):
                    def etfn(e, mc=mc, fc=fc):
                        e.matmul(t["ps_mm0"][:], wrk[:, ts(mc, 128)],
                                 ck[:, ts(fc, 512)], start=True, stop=False)
                        return e.matmul(t["ps_mm0"][:], wik[:, ts(mc, 128)],
                                        sk[:, ts(fc, 512)], start=False, stop=True)
                    sq2.on("tensor", etfn)
                    if kc == 0:
                        sq2.on("vector", lambda e, mc=mc, fc=fc: e.tensor_copy(
                            EtS(mc)[:, ts(fc, 512)], t["ps_mm0"][:]))
                    else:
                        sq2.on("vector", lambda e, mc=mc, fc=fc: e.tensor_tensor(
                            EtS(mc)[:, ts(fc, 512)], EtS(mc)[:, ts(fc, 512)],
                            t["ps_mm0"][:], ALU.add))
            sq2.on("vector", lambda e: e.tensor_scalar(wik, wik, -1.0, None, ALU.mult))
            for mc in range(8):
                for fc in range(2):
                    def ftfn(e, mc=mc, fc=fc):
                        e.matmul(t["ps_mm0"][:], wrk[:, ts(mc, 128)],
                                 sk[:, ts(fc, 512)], start=True, stop=False)
                        return e.matmul(t["ps_mm0"][:], wik[:, ts(mc, 128)],
                                        ck[:, ts(fc, 512)], start=False, stop=True)
                    sq2.on("tensor", ftfn)
                    if kc == 0:
                        sq2.on("vector", lambda e, mc=mc, fc=fc: e.tensor_copy(
                            FtS(mc)[:, ts(fc, 512)], t["ps_mm0"][:]))
                    else:
                        sq2.on("vector", lambda e, mc=mc, fc=fc: e.tensor_tensor(
                            FtS(mc)[:, ts(fc, 512)], FtS(mc)[:, ts(fc, 512)],
                            t["ps_mm0"][:], ALU.add))
        for mc in range(8):
            for fc in range(2):
                for kc in range(8):
                    sq2.on("sync", lambda e, mc=mc, kc=kc: e.dma_start(
                        out=t["trw0"][:], in_=trc[ts(kc, 128), ts(mc, 128)]), dma=True)
                    sq2.on("sync", lambda e, mc=mc, kc=kc: e.dma_start(
                        out=t["trw1"][:], in_=trs[ts(kc, 128), ts(mc, 128)]), dma=True)
                    def mtfn(e, fc=fc, kc=kc):
                        e.matmul(t["ps_mm0"][:], t["trw0"][:], EtS(kc)[:, ts(fc, 512)],
                                 start=(kc == 0), stop=False)
                        return e.matmul(t["ps_mm0"][:], t["trw1"][:],
                                        FtS(kc)[:, ts(fc, 512)], start=False,
                                        stop=(kc == 7))
                    sq2.on("tensor", mtfn)
                sq2.on("scalar", lambda e: e.activation(
                    out=a2F[:, 0:512], in_=t["ps_mm0"][:], func=AF.Identity,
                    scale=INV_D))
                sq2.on("sync", lambda e, mc=mc, fc=fc: e.dma_start(
                    out=st_mt[ts(mc, 128), ts(fc, 512)], in_=a2F[:, 0:512]),
                    dma=True)
        for bi in range(8):
            for bj in range(8):
                sq2.on("sync", lambda e, bi=bi, bj=bj: e.dma_start(
                    out=t["trw0"][:], in_=st_mt[ts(bi, 128), ts(bj, 128)]), dma=True)
                sq2.on("tensor", lambda e: e.transpose(ps_tr, t["trw0"][:], ident))
                sq2.on("vector", lambda e, bi=bi, bj=bj: e.tensor_copy(
                    mt_bf[:, bj * 1024 + bi * 128:bj * 1024 + (bi + 1) * 128], ps_tr))
        sq2.on("gpsimd", lambda e: e.collective_compute(
            "AllReduce", ALU.add, ins=[y_part[:]], outs=[y_cc[:]],
            replica_groups=[[0, 1], [2, 3], [4, 5], [6, 7]]))
        for kc in range(8):
            sq2.on("sync", lambda e, kc=kc: e.dma_start(
                out=wout_bf[:, ts(kc, 1024)], in_=w_out[ts(kc, 128), :]), dma=True)
        for blk in range(16):
            for kc in range(8):
                sq2.on("sync", lambda e, blk=blk, kc=kc: e.dma_start(
                    out=ygk, in_=y_cc[ts(kc, 128), ts(blk, 512)]), dma=True)
                sq2.on("sync", lambda e, blk=blk, kc=kc: e.dma_start(
                    out=szk, in_=st_sz[ts(kc, 128), ts(blk, 512)]), dma=True)
                sq2.on("vector", lambda e, kc=kc: e.tensor_tensor(
                    silp3[:, ts(kc, 512)], ygk, szk, ALU.mult))
            for mc in range(8):
                def wofn(e, mc=mc):
                    for kc in range(8):
                        inst = e.matmul(
                            t["ps_mm0"][:],
                            wout_bf[:, kc * 1024 + mc * 128:kc * 1024 + (mc + 1) * 128],
                            silp3[:, ts(kc, 512)], start=(kc == 0), stop=(kc == 7))
                    return inst
                sq2.on("tensor", wofn)
                sq2.on("scalar", lambda e, mc=mc: e.activation(
                    out=att8[:, ts(mc, 512)], in_=t["ps_mm0"][:], func=AF.Identity,
                    bias=t["bout_sb"][:, mc:mc + 1]))
            for tk in range(4):
                sq2.on("sync", lambda e, blk=blk, tk=tk: e.dma_start(
                    out=xt3, in_=x_in[blk * 512 + tk * 128:blk * 512 + (tk + 1) * 128,
                                      :]), dma=True)
                for kc in range(8):
                    sq2.on("tensor", lambda e, tk=tk, kc=kc: e.transpose(
                        ps_tr, att8[:, kc * 512 + tk * 128:kc * 512 + (tk + 1) * 128], ident))
                    sq2.on("vector", lambda e, tk=tk, kc=kc: e.tensor_tensor(
                        xmid[:, tk * 1024 + kc * 128:tk * 1024 + (kc + 1) * 128], ps_tr,
                        xt3[:, ts(kc, 128)], ALU.add))
                xmv = xmid[:, ts(tk, 1024)]
                sq2.on("vector", lambda e, xmv=xmv: e.reduce_sum(
                    t["mvt"][:, 0:1], xmv, axis=AXX))
                sq2.on("vector", lambda e: e.tensor_scalar(
                    t["mvt"][:, 1:2], t["mvt"][:, 0:1], INV_D, None, ALU.mult))
                sq2.on("vector", lambda e, xmv=xmv: e.tensor_scalar(
                    xc3, xmv, t["mvt"][:, 1:2], None, ALU.subtract))
                sq2.on("scalar", lambda e: e.activation(out=xl3, in_=xc3,
                                                        func=AF.Square))
                sq2.on("vector", lambda e: e.reduce_sum(t["mvt"][:, 2:3], xl3,
                                                        axis=AXX))
                sq2.on("vector", lambda e: e.tensor_scalar(
                    t["mvt"][:, 3:4], t["mvt"][:, 2:3], INV_D, None, ALU.mult))
                sq2.on("scalar", lambda e: e.activation(
                    out=t["rstd"][:], in_=t["mvt"][:, 3:4], func=AF.Ln, bias=1e-5))
                sq2.on("scalar", lambda e: e.activation(
                    out=t["rstd"][:], in_=t["rstd"][:], func=AF.Exp, scale=-0.5))
                sq2.on("vector", lambda e: e.tensor_scalar(
                    xl3, xc3, t["rstd"][:, 0:1], None, ALU.mult))
                for kc in range(8):
                    sq2.on("tensor", lambda e, kc=kc: e.transpose(
                        ps_tr, xl3[:, kc * 128:(kc + 1) * 128], ident))
                    sq2.on("vector", lambda e, tk=tk, kc=kc: e.tensor_copy(
                        xl2b[:, kc * 512 + tk * 128:kc * 512 + (tk + 1) * 128], ps_tr))
            for mc in range(8):
                def fffn(e, mc=mc):
                    for kc in range(8):
                        inst = e.matmul(
                            t["ps_mm0"][:],
                            mt_bf[:, kc * 1024 + mc * 128:kc * 1024 + (mc + 1) * 128],
                            xl2b[:, ts(kc, 512)], start=(kc == 0), stop=(kc == 7))
                    return inst
                sq2.on("tensor", fffn)
                sq2.on("scalar", lambda e, mc=mc: e.activation(
                    out=ff8[:, ts(mc, 512)], in_=t["ps_mm0"][:], func=AF.Identity,
                    bias=t["fb_sb"][:, mc:mc + 1]))
            for tk in range(4):
                for kc in range(8):
                    sq2.on("tensor", lambda e, tk=tk, kc=kc: e.transpose(
                        ps_tr, ff8[:, kc * 512 + tk * 128:kc * 512 + (tk + 1) * 128], ident))
                    sq2.on("vector", lambda e, tk=tk, kc=kc: e.tensor_tensor(
                        xl3[:, kc * 128:(kc + 1) * 128], ps_tr,
                        xmid[:, tk * 1024 + kc * 128:tk * 1024 + (kc + 1) * 128], ALU.add))
                sq2.on("sync", lambda e, blk=blk, tk=tk: e.dma_start(
                    out=out[blk * 512 + tk * 128:blk * 512 + (tk + 1) * 128, :],
                    in_=xl3), dma=True)
        G_POST = sq2.pos

        @block.sync
        def _(sync):
            sq.emit("sync", sync, g)
            sync.wait_ge(g, G_P1)
            chunk_dmas(sync, cbufs("A"), 0)
            chunk_dmas(sync, cbufs("B"), 1)
            mA = Mono(nc, sync, svA, 3, TC * V_INCS)
            mB = Mono(nc, sync, svB, 3, TC * V_INCS)
            with sync.Fori(0, npair) as cp:
                mA.wait(sync, TC * V_INCS + OFF_SV)
                chunk_dmas(sync, cbufs("A"), cp * 2 + 2)
                mB.wait(sync, TC * V_INCS + OFF_SV)
                chunk_dmas(sync, cbufs("B"), cp * 2 + 3)
            mA.check(); mB.check()
            sq2.emit("sync", sync, g2)
            sync.wait_ge(g2, G_POST)

        @block.vector
        def _(v):
            sq.emit("vector", v, g)
            v.wait_ge(g, G_P1)
            for sem, n in ((svA, OFF_SV), (svB, OFF_SV), (spe, 1), (pg, 4),
                           (sye, 1)):
                for _ in range(n):
                    v.memset(t["scr2"][:, 0:1], 0.0).then_inc(sem, 1)
            m_ssA = Mono(nc, v, ssA, 0, TC * S_INCS)
            m_ssB = Mono(nc, v, ssB, 0, TC * S_INCS)
            m_spe = Mono(nc, v, spe, 0, 2 * TC * PE_INCS)
            m_pg = Mono(nc, v, pg, 4, 2 * PREP_INCS)
            m_sdi = Mono(nc, v, sdi, 0, 2 * DMAS_PER_CHUNK * 16)
            with v.Fori(0, npair) as cp:
                for par in (0, 1):
                    sv = svA if par == 0 else svB
                    cb = cbufs("A" if par == 0 else "B")
                    ch = par
                    emit_prep("vector", v, cb, ch * PREP_INCS,
                              (ch + 1) * DMAS_PER_CHUNK * 16, m_pg, m_sdi)
                    m_ss = m_ssA if par == 0 else m_ssB
                    for k in range(TC):
                        emit_step_v(v, t, cons_v, spe, step_bulk(cb, k), sv, k,
                                    t["pyt0"] if k % 2 == 0 else t["pyt1"],
                                    ch, m_ss, m_spe, m_pg)
            for m in (m_ssA, m_ssB, m_spe, m_pg, m_sdi):
                m.check()
            v.wait_ge(sdi, (2 * npair + 2) * DMAS_PER_CHUNK * 16)
            v.wait_ge(pg, 2 * npair * PREP_INCS + 4)
            sq2.emit("vector", v, g2)
            v.wait_ge(g2, G_POST)

        @block.scalar
        def _(s):
            sq.emit("scalar", s, g)
            s.wait_ge(g, G_P1)
            m_svA = Mono(nc, s, svA, 0, TC * V_INCS)
            m_svB = Mono(nc, s, svB, 0, TC * V_INCS)
            m_speS = Mono(nc, s, spe, 2, 2 * TC * PE_INCS)
            m_sdyS = Mono(nc, s, sdy, 16, 32)
            m_pgS = Mono(nc, s, pg, 0, 2 * PREP_INCS)
            m_sdiS = Mono(nc, s, sdi, 0, 2 * DMAS_PER_CHUNK * 16)
            with s.Fori(0, npair) as cp:
                for par in (0, 1):
                    ss = ssA if par == 0 else ssB
                    ch = par
                    emit_prep("scalar", s, cbufs("A" if par == 0 else "B"),
                              ch * PREP_INCS, (ch + 1) * DMAS_PER_CHUNK * 16,
                              m_pgS, m_sdiS)
                    m_sv = m_svA if par == 0 else m_svB
                    ysb = t["ySB0"]
                    for k in range(TC):
                        emit_step_s(s, t, ss, k, m_sv)
                        if k % 8 == 7:
                            s8 = k // 8
                            m_speS.wait(s, ch * TC * PE_INCS
                                        + (k + 1) * PE_INCS + 2)
                            if s8 == 0:
                                m_sdyS.wait(s, 16 * ch + 32)
                            s.activation(
                                out=ysb[:].rearrange("dh (dl t) -> dh dl t",
                                                     dl=64)[:, :,
                                                            s8 * 8:(s8 + 1) * 8],
                                in_=t["ps_y" + str(s8 % 2)][:].rearrange(
                                    "dh (t dl) -> dh dl t", t=8),
                                func=AF.Identity).then_inc(sye, 1)
                m_pgS.wait(s, 2 * PREP_INCS)
            for m in (m_svA, m_svB, m_speS, m_sdyS, m_pgS, m_sdiS):
                m.check()
            sq2.emit("scalar", s, g2)
            s.wait_ge(g2, G_POST)

        @block.tensor
        def _(pe):
            sq.emit("tensor", pe, g)
            pe.wait_ge(g, G_P1)
            m_svAp = Mono(nc, pe, svA, 2, TC * V_INCS)
            m_svBp = Mono(nc, pe, svB, 2, TC * V_INCS)
            m_syeP = Mono(nc, pe, sye, 0, 2 * 8)
            pe.matmul(t["ps_hc"][:, 0:2], Wmean, t["crci"][:],
                      start=True, stop=False)
            pe.matmul(t["ps_hc"][:, 0:2], c1t[0:1, 534:662],
                      c1t[0:1, 662:664], start=False,
                      stop=True).then_inc(spe, 1)
            with pe.Fori(0, npair) as cp:
                for par in (0, 1):
                    ch = par
                    m_sv = m_svAp if par == 0 else m_svBp
                    for k in range(TC):
                        gslot = ch * 8 + k // 8
                        pyt = t["pyt0"] if k % 2 == 0 else t["pyt1"]
                        psy = t["ps_y0"] if (k // 8) % 2 == 0 else t["ps_y1"]
                        m_sv.wait(pe, k * V_INCS + 7 + OFF_SV)
                        pe.matmul(t["ps_hc"][:, 0:2], Wmean, t["crci"][:],
                                  start=True, stop=False)
                        pe.matmul(t["ps_hc"][:, 0:2], c1t[0:1, 534:662],
                                  c1t[0:1, 662:664], start=False,
                                  stop=True).then_inc(spe, 1)
                        m_sv.wait(pe, k * V_INCS + 8 + OFF_SV)
                        m_syeP.wait(pe, gslot + 1)
                        pe.matmul(psy[:, ts(k % 8, 64)], W4, pyt[:],
                                  start=(k % 8 == 0), stop=(k % 8 == 7)
                                  ).then_inc(spe, 1)
            for m in (m_svAp, m_svBp, m_syeP):
                m.check()
            sq2.emit("tensor", pe, g2)
            pe.wait_ge(g2, G_POST)

        @block.gpsimd
        def _(gp):
            sq.emit("gpsimd", gp, g)
            gp.wait_ge(g, G_P1)
            gp.dma_start(out=t["mvt"][:, 0:2], in_=cons1[:, 144:146]).then_inc(sdy, 16)
            gp.dma_start(out=t["mvt"][:, 2:4], in_=cons1[:, 144:146]).then_inc(sdy, 16)
            m_syeG = Mono(nc, gp, sye, 1, 2 * 8)
            with gp.Fori(0, npair) as cp:
                for par in (0, 1):
                    ch = par
                    ysb = t["ySB0"]
                    m_syeG.wait(gp, (ch + 1) * 8 + 1)
                    gp.dma_start(
                        out=y_part.rearrange("(dh dl) s -> dh dl s", dh=16)[
                            :, :, ds((cp * 2 + ch) * TC, TC)],
                        in_=ysb[:]
                    ).then_inc(sdy, 16)
                    if dbg:
                        gp.dma_start(
                            out=dbg_outs["d_Htr"][:, ds((cp * 2 + ch) * 128, 128)],
                            in_=t["H"][:]).then_inc(sdi, 16)
            m_syeG.check()
            gp.wait_ge(sdy, 32 + 2 * npair * 16)
            sq2.emit("gpsimd", gp, g2)
            gp.wait_ge(g2, G_POST)
            if dbg:
                dsem = sdi
                nd = 0
                for nm, src_ap in (("d_xz", st_xz[:, 0:512]),
                                   ("d_xp", st_xp[:, 0:512]),
                                   ("d_dt", st_dt[:, 0:512]),
                                   ("d_u", st_u[:, 0:512]),
                                   ("d_sz", st_sz[:, 0:512]),
                                   ("d_y", y_part[:, 0:512]),
                                   ("d_ycc", y_cc[:, 0:512]),
                                   ("d_mt", st_mt[:, 0:512])):
                    gp.dma_start(out=dbg_outs[nm][:], in_=src_ap).then_inc(dsem, 16)
                    nd += 1
                gp.dma_start(out=dbg_outs["d_xln"][:],
                             in_=st_xlnT[:, 0:512]).then_inc(dsem, 16)
                nd += 1
                for nm2 in ("H", "tsq", "crci", "EXPC", "s3",
                            "e2", "LNB", "LNBo", "qcmt", "s2",
                            "pw", "lg", "LNA", "LNAo", "dotv",
                            "rn", "pyt0", "pyt1"):
                    gp.dma_start(out=dbg_outs["ds_" + nm2][:],
                                 in_=t[nm2][:]).then_inc(dsem, 16)
                    nd += 1
                gp.wait_ge(dsem, (NCHUNK + 2) * DMAS_PER_CHUNK * 16 + nd * 16)

    for hx in reversed(ctxs):
        hx.__exit__(None, None, None)
    return nc


_NC_CACHE = None


def kernel(**inputs):
    global _NC_CACHE
    if _NC_CACHE is None:
        _NC_CACHE = build_nc()
    in_maps = [make_inputs(c, inputs) for c in range(8)]
    res = run_bass_kernel_spmd(_NC_CACHE, in_maps, core_ids=list(range(8)))
    outp = np.zeros((4, S, D), np.float32)
    for b in range(4):
        outp[b] = res.results[2 * b]["out"]
    return outp
